# version 11
# baseline (speedup 1.0000x reference)
import numpy as np
import ml_dtypes
import concourse.bass as bass
import concourse.mybir as mybir
from concourse.bass import IndirectOffsetOnAxis
from concourse.bass_utils import run_bass_kernel_spmd
from contextlib import ExitStack

F32 = mybir.dt.float32
BF16 = mybir.dt.bfloat16
U32 = mybir.dt.uint32
AF = mybir.ActivationFunctionType
ALU = mybir.AluOpType
AX = mybir.AxisListType.X

NCORES = 8
NTOK = 4096
LT = 256
NLT = NTOK // LT
CH = 128
DM = 1024
DI = 2048
DST = 128
NH = 32
CONVD = 2304
NCT = 18
DIP = 4480
WXW = 2336
EPS = 1e-5
E = 32
CAP = 5120
C_LOC = 1024
EB_ROWS = E * C_LOC + 128
DUMP = 10_000_000.0

ENGS = ("sync", "gpsimd", "tensor", "vector", "scalar")
SERIAL_ENGS = {"vector": "vs", "scalar": "ss"}

DMA_OWNER = {"ld": "sync", "gq": "gpsimd", "sq": "scalar"}


class Prog:
    def __init__(self, nc, es):
        self.nc = nc
        self.ops = {e: [] for e in ENGS}
        self.cnt = {}
        self.sems = {}
        self.es = es
        self.last_use = {}
        self.last_throttle = {}
        self.bankdep = {}
        self.last_op = {}

    def sem(self, name):
        if name not in self.sems:
            self.sems[name] = self.es.enter_context(self.nc.semaphore(name))
            self.cnt[name] = 0
        return self.sems[name]

    def val(self, name):
        return self.cnt.get(name, 0)

    def op(self, eng, fn, sem=None, n=1):
        if eng in SERIAL_ENGS:
            if sem is None:
                sem = SERIAL_ENGS[eng]
            le = self.last_op.get(eng)
            if le is not None:
                self.ops[eng].append(lambda E, s=le[0], v=le[1]: E.wait_ge(s, v))
            s = self.sem(sem)
            self.cnt[sem] += n
            self.ops[eng].append(lambda E, fn=fn, s=s, n=n: fn(E).then_inc(s, n))
            self.last_op[eng] = (s, self.cnt[sem])
            return self.cnt[sem]
        if sem is None:
            self.ops[eng].append(fn)
            return None
        s = self.sem(sem)
        if sem in DMA_OWNER:
            lu = self.last_use.get(sem, 0)
            if lu > self.last_throttle.get(sem, 0):
                owner = DMA_OWNER[sem]
                self.ops[owner].append(lambda E, s=s, v=lu: E.wait_ge(s, v))
                self.last_throttle[sem] = lu
        self.cnt[sem] += n
        self.ops[eng].append(lambda E, fn=fn, s=s, n=n: fn(E).then_inc(s, n))
        return self.cnt[sem]

    def guard(self, bank):
        for s_, v_ in self.bankdep.get(bank, {}).items():
            self.wait("tensor", s_, v_)

    def evac(self, bank, sem):
        self.bankdep.setdefault(bank, {})[sem] = self.val(sem)

    def wait(self, eng, sem, v=None):
        if v is None:
            v = self.val(sem)
        if v <= 0:
            return
        if sem in DMA_OWNER:
            self.last_use[sem] = max(self.last_use.get(sem, 0), v)
        s = self.sem(sem)
        self.ops[eng].append(lambda E, s=s, v=v: E.wait_ge(s, v))

    def run(self, block):
        P = self

        @block.sync
        def _(e):
            for f in P.ops["sync"]:
                f(e)

        @block.gpsimd
        def _(e):
            for f in P.ops["gpsimd"]:
                f(e)

        @block.tensor
        def _(e):
            for f in P.ops["tensor"]:
                f(e)

        @block.vector
        def _(e):
            for f in P.ops["vector"]:
                f(e)

        @block.scalar
        def _(e):
            for f in P.ops["scalar"]:
                f(e)


def build():
    nc = bass.Bass("TRN2", target_bir_lowering=False, debug=False)
    ap = {}

    def din(name, shape, dt=F32):
        ap[name] = nc.dram_tensor(name, shape, dt, kind="ExternalInput").ap()
        return ap[name]

    def dout(name, shape, dt=F32):
        ap[name] = nc.dram_tensor(name, shape, dt, kind="ExternalOutput").ap()
        return ap[name]

    xo = din("xo", [NTOK + 3, DM])
    xp = din("xp", [NTOK + 3, DM])
    win = din("win", [DM, DIP])
    convd = din("convd", [4 * NCT * 128, 128])
    din("convb", [128, NCT])
    din("dtb", [32, 1])
    din("alog", [32, 1])
    din("dbc", [128, DI])
    din("rmsbc", [128, DI])
    din("lngbc", [128, DM])
    din("lnbbc", [128, DM])
    wout = din("wout", [DI, DM])
    din("selc", [128, 1])
    din("diagm", [128, 8 * 128], BF16)
    din("bigm", [128, 128])
    din("ones128", [128, 128])
    din("identf", [128, 128])
    din("identb", [128, 128], BF16)
    din("onesr", [1, 128])
    din("onescol", [128, 1])
    din("oh127", [128, 1])
    wgt = din("wgt", [DM, E])
    w1 = din("w1", [E * DM, DM])
    w3 = din("w3", [E * DM, DM])
    w2 = din("w2", [E * DM, DM])
    sw1 = din("sw1", [DM, DM])
    sw3 = din("sw3", [DM, DM])
    sw2 = din("sw2", [DM, DM])
    din("lstT", [128, 128])
    din("clocb", [128, E])
    din("iota8", [128, 8])
    din("maskc", [8, 1])
    dout("out", [NTOK, DM])

    wz_i = nc.dram_tensor("wz_i", [DM, DI], BF16)
    wo_i = nc.dram_tensor("wo_i", [DI, DM], BF16)
    w1_i = nc.dram_tensor("w1_i", [E * 128, 8 * DM], BF16)
    w3_i = nc.dram_tensor("w3_i", [E * 128, 8 * DM], BF16)
    w2_i = nc.dram_tensor("w2_i", [E * 128, 8 * DM], BF16)
    ebuf = nc.dram_tensor("ebuf", [EB_ROWS, DM], BF16)
    rec = nc.dram_tensor("rec", [EB_ROWS, 2], F32)
    xnw = nc.dram_tensor("xnw", [NTOK, DM], F32)
    cntin = nc.dram_tensor("cntin", [1, E], F32)
    cnt8o = nc.dram_tensor("cnt8o", [8, E], F32)

    es = ExitStack()
    S = es.enter_context
    with es:
        identbs = S(nc.sbuf_tensor("identbs", [128, 128], BF16))
        onesrs = S(nc.sbuf_tensor("onesrs", [1, 128], F32))
        onecs = S(nc.sbuf_tensor("onecs", [128, 1], F32))
        epsc = S(nc.sbuf_tensor("epsc", [128, 1], F32))
        cols = S(nc.sbuf_tensor("cols", [128, 8], F32))
        hwork = S(nc.sbuf_tensor("hwork", [128, DM], F32))
        arena = S(nc.sbuf_tensor("arena", [128, 3072], F32))
        zbf = S(nc.sbuf_tensor("zbf", [128, 2048], BF16))
        zerot = S(nc.sbuf_tensor("zerot", [128, 576], F32))

        P = Prog(nc, es)
        ld, gq, sq, ts, vs, ss, ost = "ld", "gq", "sq", "ts", "vs", "ss", "ld"

        esA = ExitStack()
        A = esA.enter_context
        wxbc = A(nc.sbuf_tensor("wxbc", [128, 8, WXW], BF16))
        cdg = A(nc.sbuf_tensor("cdg", [128, 4 * NCT, 128], BF16))
        cbs = A(nc.sbuf_tensor("cbs", [128, NCT], F32))
        dtbs = A(nc.sbuf_tensor("dtbs", [32, 1], F32))
        anegs = A(nc.sbuf_tensor("anegs", [32, 1], F32))
        dbcs = A(nc.sbuf_tensor("dbcs", [128, DI], BF16))
        rmsbs = A(nc.sbuf_tensor("rmsbs", [128, DI], BF16))
        lngs = A(nc.sbuf_tensor("lngs", [128, DM], BF16))
        lnbs = A(nc.sbuf_tensor("lnbs", [128, DM], BF16))
        sels = A(nc.sbuf_tensor("sels", [128, 1], F32))
        diagms = A(nc.sbuf_tensor("diagms", [128, 8 * 128], BF16))
        bigms = A(nc.sbuf_tensor("bigms", [128, 128], F32))
        ones128s = A(nc.sbuf_tensor("ones128s", [128, 128], F32))
        identfs = A(nc.sbuf_tensor("identfs", [128, 128], F32))
        oh127s = A(nc.sbuf_tensor("oh127s", [128, 1], F32))
        wzb = [A(nc.sbuf_tensor(f"wzb{i}", [128, 8, 512], BF16)) for i in range(2)]
        wob = [A(nc.sbuf_tensor(f"wob{i}", [128, 512], BF16)) for i in range(2)]
        xtile = A(nc.sbuf_tensor("xtile", [128, LT // 128, DM], F32))
        lnxtok = A(nc.sbuf_tensor("lnxtok", [128, LT // 128, DM], BF16))
        lnxT = A(nc.sbuf_tensor("lnxT", [128, 8, LT], BF16))
        halT = A(nc.sbuf_tensor("halT", [128, 8, 3], BF16))
        sdz = A(nc.sbuf_tensor("sdz", [128, LT // 128, DI], BF16))
        xbcT = A(nc.sbuf_tensor("xbcT", [128, NCT, LT + 3], BF16))
        dtpsT = A(nc.sbuf_tensor("dtpsT", [32, LT], F32))
        xchTs2 = A(nc.sbuf_tensor("xchTs2", [128, 2, LT], BF16))
        btct = A(nc.sbuf_tensor("btct", [128, 2, LT], BF16))
        xtok = A(nc.sbuf_tensor("xtok", [128, LT // 128, DI], BF16))
        btok = A(nc.sbuf_tensor("btok", [128, LT // 128, DST], BF16))
        xprime = A(nc.sbuf_tensor("xprime", [128, DI], BF16))
        dtT = A(nc.sbuf_tensor("dtT", [32, LT], F32))
        acT = A(nc.sbuf_tensor("acT", [32, LT], F32))
        lndT = A(nc.sbuf_tensor("lndT", [32, LT], F32))
        biasT = A(nc.sbuf_tensor("biasT", [32, LT], F32))
        wdT = A(nc.sbuf_tensor("wdT", [32, LT], F32))
        stk = A(nc.sbuf_tensor("stk", [96, 128], F32))
        tokc = A(nc.sbuf_tensor("tokc", [128, 96], F32))
        utok = A(nc.sbuf_tensor("utok", [128, 32], F32))
        drow = A(nc.sbuf_tensor("drow", [1, 32], F32))
        mpall = A(nc.sbuf_tensor("mpall", [128, 8 * 128], BF16))
        gev = A(nc.sbuf_tensor("gev", [128, 128], BF16))
        prev = A(nc.sbuf_tensor("prev", [128, DI], F32))
        prevb = A(nc.sbuf_tensor("prevb", [128, DI], BF16))
        ysb = A(nc.sbuf_tensor("ysb", [128, DI], F32))
        expq = A(nc.sbuf_tensor("expq", [128, 1024], BF16))
        ynbf = A(nc.sbuf_tensor("ynbf", [128, DI], BF16))
        ynT = A(nc.sbuf_tensor("ynT", [128, 16, LT], BF16))

        pA = A(nc.psum_tensor("pA", [128, 512], F32))
        pB = A(nc.psum_tensor("pB", [128, 512], F32))
        pT = A(nc.psum_tensor("pT", [128, 512], BF16))
        pG = A(nc.psum_tensor("pG", [128, 128], F32))
        pR = A(nc.psum_tensor("pR", [128, 1024], F32))
        pY = A(nc.psum_tensor("pY", [128, 512], F32))
        pM = A(nc.psum_tensor("pM", [128, 512], F32))

        P.op("vector", lambda e: e.memset(zbf[:], 0.0), vs)
        P.op("vector", lambda e: e.memset(zerot[:], 0.0), vs)
        P.op("vector", lambda e: e.memset(epsc[:], EPS), vs)
        P.op("vector", lambda e: e.memset(cols[:], 0.0), vs)
        P.op("vector", lambda e: e.memset(prev[:], 0.0), vs)
        P.wait("scalar", vs, P.val(vs))
        for r in range(0, EB_ROWS - 255, 256):
            P.op("scalar", lambda e, r=r: e.dma_start(
                out=ebuf[r:r + 256, :].rearrange("(a p) n -> p a n", p=128),
                in_=zbf[:, :].rearrange("p (a n) -> p a n", a=2)), sq, 16)
        P.op("scalar", lambda e: e.dma_start(
            out=ebuf[EB_ROWS - 128:EB_ROWS, :], in_=zbf[:, 0:1024]), sq, 16)
        P.op("scalar", lambda e: e.dma_start(
            out=rec[0:EB_ROWS, :].rearrange("(a p) n -> p a n", p=128),
            in_=zerot[:, 0:(EB_ROWS // 128) * 2].rearrange("p (a n) -> p a n", n=2)), sq, 16)
        zval = P.val(sq)

        for name, dst in [("dbc", dbcs), ("rmsbc", rmsbs), ("lngbc", lngs), ("lnbbc", lnbs)]:
            P.op("gpsimd", lambda e, a=ap[name], d=dst: e.dma_start(out=d[:], in_=a[:, :]), gq, 16)
        for name, dst in [("convb", cbs), ("dtb", dtbs),
                          ("selc", sels), ("diagm", diagms), ("bigm", bigms),
                          ("ones128", ones128s), ("identf", identfs),
                          ("identb", identbs), ("onesr", onesrs), ("onescol", onecs),
                          ("oh127", oh127s), ("alog", anegs)]:
            P.op("sync", lambda e, a=ap[name], d=dst: e.dma_start(out=d[:], in_=a[:, :]), ld, 16)
        P.op("gpsimd", lambda e: e.dma_start(
            out=wxbc[:], in_=win[:, DI:DI + WXW].rearrange("(k p) n -> p k n", p=128)), gq, 16)
        P.op("gpsimd", lambda e: e.dma_start(
            out=cdg[:], in_=convd[:, :].rearrange("(t p) n -> p t n", p=128)), gq, 16)
        P.op("gpsimd", lambda e: e.dma_start(out=wz_i[:, :], in_=win[:, 0:DI]), gq, 16)
        P.op("gpsimd", lambda e: e.dma_start(out=wo_i[:, :], in_=wout[:, :]), gq, 16)
        P.wait("sync", gq, P.val(gq))
        P.wait("scalar", ld, P.val(ld))
        P.op("scalar", lambda e: e.activation(out=anegs[:], in_=anegs[:], func=AF.Exp), ss)
        P.wait("vector", ss, P.val(ss))
        P.wait("vector", gq, P.val(gq))
        P.op("vector", lambda e: e.tensor_scalar_mul(out=anegs[:], in0=anegs[:], scalar1=-1.0), vs)
        P.wait("tensor", ld, P.val(ld))
        P.wait("tensor", gq, P.val(gq))
        P.wait("tensor", vs, P.val(vs))
        P.wait("scalar", vs, P.val(vs))
        stage_q = []
        for eix in range(E):
            for src, dst in ((w1, w1_i), (w3, w3_i), (w2, w2_i)):
                stage_q.append(lambda e, src=src, dst=dst, eix=eix: e.dma_start(
                    out=dst[eix * 128:(eix + 1) * 128, :].rearrange("p (k n) -> p k n", k=8),
                    in_=src[eix * DM:(eix + 1) * DM, :].rearrange("(k p) n -> p k n", p=128)))

        def drain_stage(nmax):
            for _ in range(nmax):
                if stage_q:
                    P.op("gpsimd", stage_q.pop(0), gq, 16)

        def ln_rows(src_ap, dst_ap, r):
            P.op("vector", lambda e: e.tensor_reduce(out=cols[0:r, 0:1], in_=src_ap, axis=AX, op=ALU.add), vs)
            P.op("vector", lambda e: e.tensor_scalar_mul(out=cols[0:r, 0:1], in0=cols[0:r, 0:1], scalar1=1.0 / DM), vs)
            P.op("vector", lambda e: e.tensor_scalar(
                out=arena[0:r, 2048:3072], in0=src_ap, scalar1=cols[0:r, 0:1], scalar2=None, op0=ALU.subtract), vs)
            P.wait("scalar", vs, P.val(vs))
            P.op("scalar", lambda e: e.activation(
                out=arena[0:r, 1024:2048], in_=arena[0:r, 2048:3072], func=AF.Square, accum_out=cols[0:r, 1:2]), ss)
            P.op("scalar", lambda e: e.activation(
                out=cols[0:r, 2:3], in_=cols[0:r, 1:2], func=AF.Ln, scale=1.0 / DM, bias=epsc[0:r, 0:1]), ss)
            P.op("scalar", lambda e: e.activation(
                out=cols[0:r, 2:3], in_=cols[0:r, 2:3], func=AF.Exp, scale=-0.5), ss)
            P.wait("vector", ss, P.val(ss))
            P.op("vector", lambda e: e.tensor_scalar(
                out=dst_ap, in0=arena[0:r, 2048:3072], scalar1=cols[0:r, 2:3], scalar2=None, op0=ALU.mult), vs)

        def transpose_128(src_ap, dst_ap, cols_n=128):
            P.wait("tensor", vs, P.val(vs))
            P.op("tensor", lambda e: e.transpose(out=pT[:, 0:cols_n], in_=src_ap, identity=identbs[0:cols_n, 0:cols_n]), ts)
            P.wait("vector", ts, P.val(ts))
            P.op("vector", lambda e: e.tensor_copy(out=dst_ap, in_=pT[:, 0:cols_n]), vs)

        def transpose_batch(srcs, dst3):
            n = len(srcs)
            P.wait("tensor", vs, P.val(vs))
            for i, s in enumerate(srcs):
                P.op("tensor", lambda e, s=s, i=i: e.transpose(
                    out=pT[:, i * 128:(i + 1) * 128], in_=s, identity=identbs[:]),
                    ts if i == n - 1 else None)
            P.wait("vector", ts, P.val(ts))
            P.op("vector", lambda e, dst3=dst3, n=n: e.tensor_copy(
                out=dst3, in_=pT[:, 0:n * 128].rearrange("p (k x) -> p k x", k=n)), vs)

        def halo_block(src):
            P.op("sync", lambda e: e.dma_start(out=xtile[0:3, 0, :], in_=src[0:3, :]), ld, 16)
            P.wait("vector", ld, P.val(ld))
            ln_rows(xtile[0:3, 0, :], lnxtok[0:3, 0, :], 3)
            P.wait("tensor", vs, P.val(vs))
            for k in range(8):
                transpose_128(lnxtok[0:3, 0, k * 128:(k + 1) * 128], halT[:, k, :], 3)
            P.wait("tensor", vs, P.val(vs))
            for ct in range(NCT):
                P.guard("pA")
                for k in range(8):
                    P.op("tensor", lambda e, ct=ct, k=k: e.matmul(
                        out=pA[:, 0:3], lhsT=wxbc[:, k, ct * 128:(ct + 1) * 128],
                        rhs=halT[:, k, :], start=(k == 0), stop=(k == 7)),
                        ts if k == 7 else None)
                P.wait("vector", ts, P.val(ts))
                P.op("vector", lambda e, ct=ct: e.tensor_copy(out=xbcT[:, ct, 0:3], in_=pA[:, 0:3]), vs)
                P.evac("pA", vs)

        def dt_pipeline():
            P.op("vector", lambda e: e.tensor_scalar_add(out=dtT[:], in0=dtpsT[:], scalar1=dtbs[:, 0:1]), vs)
            P.wait("scalar", vs, P.val(vs))
            P.op("scalar", lambda e: e.activation(out=lndT[:], in_=dtT[:], func=AF.Abs), ss)
            P.op("scalar", lambda e: e.activation(out=lndT[:], in_=lndT[:], func=AF.Exp, scale=-1.0), ss)
            P.op("scalar", lambda e: e.activation(out=lndT[:], in_=lndT[:], func=AF.Ln, bias=onecs[0:32, 0:1]), ss)
            P.wait("vector", ss, P.val(ss))
            P.op("vector", lambda e: e.tensor_scalar_max(out=dtT[:], in0=dtT[:], scalar1=0.0), vs)
            P.op("vector", lambda e: e.tensor_add(out=dtT[:], in0=dtT[:], in1=lndT[:]), vs)
            P.op("vector", lambda e: e.tensor_scalar(
                out=acT[:], in0=dtT[:], scalar1=anegs[:, 0:1], scalar2=None, op0=ALU.mult), vs)
            for c in range(LT // CH):
                P.op("vector", lambda e, c=c: e.tensor_tensor_scan(
                    out=acT[:, c * CH:(c + 1) * CH], data0=acT[:, c * CH:(c + 1) * CH],
                    data1=acT[:, c * CH:(c + 1) * CH], initial=0.0, op0=ALU.add, op1=ALU.bypass), vs)
            P.wait("scalar", vs, P.val(vs))
            P.op("scalar", lambda e: e.activation(out=lndT[:], in_=dtT[:], func=AF.Ln), ss)
            P.wait("vector", ss, P.val(ss))
            P.op("vector", lambda e: e.tensor_sub(out=biasT[:], in0=lndT[:], in1=acT[:]), vs)
            for c in range(LT // CH):
                P.op("vector", lambda e, c=c: e.tensor_scalar_add(
                    out=wdT[:, c * CH:(c + 1) * CH], in0=biasT[:, c * CH:(c + 1) * CH],
                    scalar1=acT[:, c * CH + CH - 1:c * CH + CH]), vs)
            P.wait("scalar", vs, P.val(vs))
            P.op("scalar", lambda e: e.activation(out=wdT[:], in_=wdT[:], func=AF.Exp), ss)
            P.wait("vector", ss, P.val(ss))

        def chunk_ssd(c, co, full):
            P.op("vector", lambda e, co=co: e.tensor_copy(out=stk[0:32, :], in_=biasT[:, co:co + CH]), vs)
            P.op("vector", lambda e, co=co: e.tensor_copy(out=stk[32:64, :], in_=acT[:, co:co + CH]), vs)
            P.op("vector", lambda e, co=co: e.tensor_copy(out=stk[64:96, :], in_=wdT[:, co:co + CH]), vs)
            P.wait("tensor", vs, P.val(vs))
            P.guard("pM")
            P.op("tensor", lambda e: e.transpose(out=pM[:, 0:96], in_=stk[:, :], identity=identfs[0:96, 0:96]), ts)
            P.wait("vector", ts, P.val(ts))
            P.op("vector", lambda e: e.tensor_copy(out=tokc[:], in_=pM[:, 0:96]), vs)
            P.evac("pM", vs)
            P.wait("scalar", vs, P.val(vs))
            P.op("scalar", lambda e: e.activation(out=utok[:], in_=tokc[:, 32:64], func=AF.Exp), ss)
            P.wait("tensor", vs, P.val(vs))
            P.guard("pG")
            P.op("tensor", lambda e: e.matmul(out=pG[0:1, 0:32], lhsT=oh127s[:, 0:1],
                                              rhs=tokc[:, 32:64], start=True, stop=True), ts)
            P.wait("scalar", ts, P.val(ts))
            P.op("scalar", lambda e: e.activation(out=drow[:], in_=pG[0:1, 0:32], func=AF.Exp), ss)
            P.evac("pG", ss)
            P.wait("vector", ss, P.val(ss))
            P.op("vector", lambda e: e.tensor_copy(
                out=arena[0:1, 1024:3072].rearrange("a (h p) -> a h p", h=32),
                in_=drow[:, :].rearrange("a h -> a h ()").broadcast_to((1, 32, 64))), vs)
            P.op("vector", lambda e, c=c: e.tensor_tensor(
                out=xprime[:].rearrange("p (h x) -> p h x", h=32),
                in0=xtok[:, c, :].rearrange("p (h x) -> p h x", h=32),
                in1=tokc[:, 64:96].rearrange("p h -> p h ()").broadcast_to((128, 32, 64)),
                op=ALU.mult), vs)

            if full:
                P.wait("tensor", vs, P.val(vs))
                P.guard("pG")
                P.op("tensor", lambda e, co=co: e.matmul(
                    out=pG[:, :], lhsT=btct[:, 0, co:co + CH], rhs=btct[:, 1, co:co + CH],
                    start=True, stop=True), ts)
                P.wait("vector", ts, P.val(ts))
                P.op("vector", lambda e: e.tensor_copy(out=gev[:], in_=pG[:, :]), vs)
                P.evac("pG", vs)
                P.op("vector", lambda e, c=c: e.tensor_tensor(
                    out=ysb[:], in0=xtok[:, c, :], in1=dbcs[:], op=ALU.mult), vs)
                for q in range(4):
                    P.op("vector", lambda e, q=q: e.tensor_tensor(
                        out=arena[:, 0:1024].rearrange("p (h l) -> p h l", h=8),
                        in0=tokc[:, 32 + q * 8:32 + (q + 1) * 8].rearrange("p h -> p h ()").broadcast_to((128, 8, 128)),
                        in1=diagms[:, :].rearrange("p (h l) -> p h l", h=8),
                        op=ALU.mult), vs)
                    P.wait("tensor", vs, P.val(vs))
                    P.guard("pR")
                    for n in range(2):
                        P.op("tensor", lambda e, n=n: e.matmul(
                            out=pR[:, n * 512:(n + 1) * 512], lhsT=ones128s[:],
                            rhs=arena[:, n * 512:n * 512 + 512], start=True, stop=False), None)
                        P.op("tensor", lambda e, q=q, n=n: e.matmul(
                            out=pR[:, n * 512:(n + 1) * 512], lhsT=identfs[:],
                            rhs=tokc[:, q * 8 + n * 4:q * 8 + (n + 1) * 4].rearrange(
                                "p h -> p h ()").broadcast_to((128, 4, 128)),
                            start=False, stop=False), None)
                        P.op("tensor", lambda e, n=n: e.matmul(
                            out=pR[:, n * 512:(n + 1) * 512], lhsT=identfs[:],
                            rhs=bigms[:, :].rearrange("p l -> p () l").broadcast_to((128, 4, 128)),
                            start=False, stop=True), ts)
                    P.wait("scalar", ts, P.val(ts))
                    P.op("scalar", lambda e: e.activation(out=expq[:], in_=pR[:, 0:1024], func=AF.Exp), ss)
                    P.evac("pR", ss)
                    P.wait("vector", ss, P.val(ss))
                    P.op("vector", lambda e, q=q: e.tensor_tensor(
                        out=mpall[:, :].rearrange("p (h l) -> p h l", h=8),
                        in0=expq[:].rearrange("p (h l) -> p h l", h=8),
                        in1=gev[:, :].rearrange("p l -> p () l").broadcast_to((128, 8, 128)),
                        op=ALU.mult), vs)
                    P.wait("tensor", vs, P.val(vs))
                    for hh in range(8):
                        h = q * 8 + hh
                        P.op("tensor", lambda e, h=h, hh=hh, c=c: e.matmul(
                            out=pY[:, hh * 64:(hh + 1) * 64], lhsT=mpall[:, hh * 128:(hh + 1) * 128],
                            rhs=xtok[:, c, h * 64:(h + 1) * 64], start=True, stop=True),
                            ts if hh == 7 else None)
                    P.wait("vector", ts, P.val(ts))
                    P.op("vector", lambda e, q=q: e.tensor_tensor(
                        out=ysb[:, q * 512:(q + 1) * 512], in0=ysb[:, q * 512:(q + 1) * 512],
                        in1=pY[:, :], op=ALU.add), vs)
                P.op("vector", lambda e: e.tensor_copy(out=prevb[:], in_=prev[:]), vs)
                P.wait("tensor", vs, P.val(vs))
                for n in range(4):
                    P.guard("pM")
                    P.op("tensor", lambda e, n=n, co=co: e.matmul(
                        out=pM[:, 0:512], lhsT=btct[:, 1, co:co + CH],
                        rhs=prevb[:, n * 512:(n + 1) * 512], start=True, stop=True), ts)
                    P.wait("vector", ts, P.val(ts))
                    P.op("vector", lambda e, n=n: e.tensor_tensor(
                        out=arena[:, 1024:1536].rearrange("p (h x) -> p h x", h=8),
                        in0=pM[:, 0:512].rearrange("p (h x) -> p h x", h=8),
                        in1=utok[:, n * 8:(n + 1) * 8].rearrange("p h -> p h ()").broadcast_to((128, 8, 64)),
                        op=ALU.mult), vs)
                    P.evac("pM", vs)
                    P.op("vector", lambda e, n=n: e.tensor_tensor(
                        out=ysb[:, n * 512:(n + 1) * 512], in0=ysb[:, n * 512:(n + 1) * 512],
                        in1=arena[:, 1024:1536], op=ALU.add), vs)

            P.wait("tensor", vs, P.val(vs))
            for n in range(4):
                P.guard("pA")
                P.guard("pB")
                P.op("tensor", lambda e, n=n: e.matmul(
                    out=pA[:, 0:512], lhsT=onesrs[:, :], rhs=arena[0:1, 1024 + n * 512:1024 + (n + 1) * 512],
                    start=True, stop=True), ts)
                P.op("tensor", lambda e, n=n, c=c: e.matmul(
                    out=pB[:, 0:512], lhsT=btok[:, c, :], rhs=xprime[:, n * 512:(n + 1) * 512],
                    start=True, stop=True), ts)
                P.wait("vector", ts, P.val(ts))
                P.op("vector", lambda e, n=n: e.tensor_tensor(
                    out=arena[:, 1536:2048], in0=prev[:, n * 512:(n + 1) * 512],
                    in1=pA[:, 0:512], op=ALU.mult), vs)
                P.evac("pA", vs)
                P.op("vector", lambda e, n=n: e.tensor_tensor(
                    out=prev[:, n * 512:(n + 1) * 512], in0=arena[:, 1536:2048],
                    in1=pB[:, 0:512], op=ALU.add), vs)
                P.evac("pB", vs)

            if full:
                P.op("vector", lambda e, c=c: e.tensor_tensor(
                    out=arena[:, 1024:3072], in0=ysb[:], in1=sdz[:, c, :], op=ALU.mult), vs)
                P.wait("scalar", vs, P.val(vs))
                P.op("scalar", lambda e: e.activation(
                    out=ysb[:], in_=arena[:, 1024:3072], func=AF.Square, accum_out=cols[:, 3:4]), ss)
                P.op("scalar", lambda e: e.activation(
                    out=cols[:, 4:5], in_=cols[:, 3:4], func=AF.Ln, scale=1.0 / DI, bias=epsc[:, 0:1]), ss)
                P.op("scalar", lambda e: e.activation(
                    out=cols[:, 4:5], in_=cols[:, 4:5], func=AF.Exp, scale=-0.5), ss)
                P.wait("vector", ss, P.val(ss))
                P.op("vector", lambda e: e.tensor_scalar(
                    out=arena[:, 1024:3072], in0=arena[:, 1024:3072], scalar1=cols[:, 4:5],
                    scalar2=None, op0=ALU.mult), vs)
                P.op("vector", lambda e: e.tensor_tensor(
                    out=ynbf[:], in0=arena[:, 1024:3072], in1=rmsbs[:], op=ALU.mult), vs)
                P.wait("tensor", vs, P.val(vs))
                for tq0 in range(0, 16, 4):
                    transpose_batch(
                        [ynbf[:, (tq0 + i) * 128:(tq0 + i + 1) * 128] for i in range(4)],
                        ynT[:, tq0:tq0 + 4, co:co + 128])

        def z_inproj(it):
            for n in range(4):
                buf = wzb[n % 2]
                if n >= 2:
                    P.wait("sync", ts, P.val(ts))
                P.op("sync", lambda e, n=n, buf=buf: e.dma_start(
                    out=buf[:], in_=wz_i[:, n * 512:(n + 1) * 512].rearrange("(k p) n -> p k n", p=128)), ld, 16)
                P.wait("tensor", ld, P.val(ld))
                for j in range(LT // 128):
                    bk, bn = (pB, "pB") if (2 * n + j) % 2 == 0 else (pA, "pA")
                    P.guard(bn)
                    for k in range(8):
                        P.op("tensor", lambda e, j=j, k=k, buf=buf, bk=bk: e.matmul(
                            out=bk[:, 0:512], lhsT=lnxT[:, k, j * 128:j * 128 + 128],
                            rhs=buf[:, k, :], start=(k == 0), stop=(k == 7)),
                            ts if k == 7 else None)
                    P.wait("scalar", ts, P.val(ts))
                    P.op("scalar", lambda e, j=j, n=n, bk=bk: e.activation(
                        out=sdz[:, j, n * 512:(n + 1) * 512], in_=bk[:, 0:512], func=AF.Silu), ss)
                    P.evac(bn, ss)
            P.wait("tensor", ss, P.val(ss))

        def outproj_tile(it):
            o = it * LT
            mmv = []
            for bn in ("pA", "pB", "pY", "pM"):
                P.guard(bn)
            for n in range(2):
                for k in range(16):
                    buf = wob[k % 2]
                    if len(mmv) >= 2:
                        P.wait("sync", ts, mmv[-2])
                    P.op("sync", lambda e, k=k, n=n, buf=buf: e.dma_start(
                        out=buf[:], in_=wo_i[k * 128:(k + 1) * 128, n * 512:(n + 1) * 512]), ld, 16)
                    P.wait("tensor", ld, P.val(ld))
                    P.op("tensor", lambda e, k=k, n=n, buf=buf: e.matmul(
                        out=(pA if n == 0 else pB)[:, 0:512],
                        lhsT=ynT[:, k, 0:128],
                        rhs=buf[:], start=(k == 0), stop=(k == 15)), None)
                    v = P.op("tensor", lambda e, k=k, n=n, buf=buf: e.matmul(
                        out=(pY if n == 0 else pM)[:, 0:512],
                        lhsT=ynT[:, k, 128:256],
                        rhs=buf[:], start=(k == 0), stop=(k == 15)), ts)
                    mmv.append(v)
            for j in range(LT // 128):
                bkA, bnA, bkB, bnB = (pA, "pA", pB, "pB") if j == 0 else (pY, "pY", pM, "pM")
                P.wait("vector", ts, P.val(ts))
                P.wait("vector", ost, P.val(ost))
                P.op("vector", lambda e, j=j, bkA=bkA: e.tensor_tensor(
                    out=hwork[:, 0:512], in0=bkA[:, 0:512], in1=lnxtok[:, j, 0:512], op=ALU.add), vs)
                P.op("vector", lambda e, j=j, bkB=bkB: e.tensor_tensor(
                    out=hwork[:, 512:1024], in0=bkB[:, 0:512], in1=lnxtok[:, j, 512:1024], op=ALU.add), vs)
                P.evac(bnA, vs)
                P.evac(bnB, vs)
                P.op("vector", lambda e: e.tensor_reduce(out=cols[:, 0:1], in_=hwork[:], axis=AX, op=ALU.add), vs)
                P.op("vector", lambda e: e.tensor_scalar_mul(out=cols[:, 0:1], in0=cols[:, 0:1], scalar1=1.0 / DM), vs)
                P.op("vector", lambda e: e.tensor_scalar(
                    out=arena[:, 0:DM], in0=hwork[:], scalar1=cols[:, 0:1], scalar2=None, op0=ALU.subtract), vs)
                P.wait("scalar", vs, P.val(vs))
                P.op("scalar", lambda e: e.activation(
                    out=arena[:, 1024:2048], in_=arena[:, 0:DM], func=AF.Square, accum_out=cols[:, 1:2]), ss)
                P.op("scalar", lambda e: e.activation(
                    out=cols[:, 2:3], in_=cols[:, 1:2], func=AF.Ln, scale=1.0 / DM, bias=epsc[:, 0:1]), ss)
                P.op("scalar", lambda e: e.activation(
                    out=cols[:, 2:3], in_=cols[:, 2:3], func=AF.Exp, scale=-0.5), ss)
                P.wait("vector", ss, P.val(ss))
                P.op("vector", lambda e: e.tensor_scalar(
                    out=arena[:, 0:DM], in0=arena[:, 0:DM], scalar1=cols[:, 2:3], scalar2=None, op0=ALU.mult), vs)
                P.op("vector", lambda e: e.tensor_tensor(out=arena[:, 0:DM], in0=arena[:, 0:DM], in1=lngs[:], op=ALU.mult), vs)
                P.op("vector", lambda e: e.tensor_tensor(out=arena[:, 0:DM], in0=arena[:, 0:DM], in1=lnbs[:], op=ALU.add), vs)
                P.op("vector", lambda e, j=j: e.tensor_tensor(
                    out=hwork[:], in0=arena[:, 0:DM], in1=xtile[:, j, :], op=ALU.add), vs)
                P.wait("sync", vs, P.val(vs))
                P.op("sync", lambda e, o=o, j=j: e.dma_start(
                    out=xnw[o + j * 128:o + (j + 1) * 128, :], in_=hwork[:]), ost, 16)

        def attn_pass(src, full):
            halo_block(src)
            for it in range(NLT):
                drain_stage(9)
                o = it * LT
                P.wait("sync", vs, P.val(vs))
                P.op("sync", lambda e, o=o: e.dma_start(
                    out=xtile[:], in_=src[3 + o:3 + o + LT, :].rearrange("(s p) n -> p s n", p=128)), ld, 16)
                P.wait("vector", ld, P.val(ld))
                for j in range(LT // 128):
                    ln_rows(xtile[:, j, :], lnxtok[:, j, :], 128)
                P.wait("tensor", vs, P.val(vs))
                for j in range(LT // 128):
                    for k0 in (0, 4):
                        transpose_batch(
                            [lnxtok[:, j, (k0 + i) * 128:(k0 + i + 1) * 128] for i in range(4)],
                            lnxT[:, k0:k0 + 4, j * 128:j * 128 + 128])
                if it > 0:
                    P.op("vector", lambda e: e.tensor_copy(
                        out=xbcT[:, 0:NCT, 0:3], in_=xbcT[:, 0:NCT, LT:LT + 3]), vs)
                P.wait("tensor", vs, P.val(vs))
                for cp in range(0, NCT, 2):
                    bk, bn = (pA, "pA") if (cp // 2) % 2 == 0 else (pB, "pB")
                    P.guard(bn)
                    for ci in range(2):
                        ct = cp + ci
                        for k in range(8):
                            P.op("tensor", lambda e, ct=ct, k=k, ci=ci, bk=bk: e.matmul(
                                out=bk[:, ci * LT:ci * LT + LT], lhsT=wxbc[:, k, ct * 128:ct * 128 + 128],
                                rhs=lnxT[:, k, :], start=(k == 0), stop=(k == 7)),
                                ts if (k == 7 and ci == 1) else None)
                    P.wait("vector", ts, P.val(ts))
                    P.op("vector", lambda e, cp=cp, bk=bk: e.tensor_copy(
                        out=xbcT[:, cp:cp + 2, 3:3 + LT],
                        in_=bk[:, 0:2 * LT].rearrange("p (c x) -> p c x", c=2)), vs)
                    P.evac(bn, vs)
                bk, bn = (pA, "pA") if (NCT // 2) % 2 == 0 else (pB, "pB")
                P.guard(bn)
                for k in range(8):
                    P.op("tensor", lambda e, k=k, bk=bk: e.matmul(
                        out=bk[0:32, 0:LT], lhsT=wxbc[:, k, NCT * 128:NCT * 128 + 32],
                        rhs=lnxT[:, k, :], start=(k == 0), stop=(k == 7)),
                        ts if k == 7 else None)
                P.wait("vector", ts, P.val(ts))
                P.op("vector", lambda e, bk=bk: e.tensor_copy(out=dtpsT[:], in_=bk[0:32, 0:LT]), vs)
                P.evac(bn, vs)
                if full:
                    z_inproj(it)
                P.wait("tensor", vs, P.val(vs))
                for cp in range(0, NCT, 2):
                    bk, bn = (pB, "pB") if (cp // 2) % 2 == 0 else (pA, "pA")
                    P.guard(bn)
                    for ci in range(2):
                        ct = cp + ci
                        for tap in range(4):
                            P.op("tensor", lambda e, ct=ct, tap=tap, ci=ci, bk=bk: e.matmul(
                                out=bk[:, ci * LT:ci * LT + LT], lhsT=cdg[:, tap * NCT + ct, :],
                                rhs=xbcT[:, ct, tap:tap + LT], start=(tap == 0), stop=(tap == 3)),
                                ts if (tap == 3 and ci == 1) else None)
                    P.wait("scalar", ts, P.val(ts))
                    if cp < 16:
                        for ci in range(2):
                            P.op("scalar", lambda e, ci=ci, cp=cp, bk=bk: e.activation(
                                out=xchTs2[:, ci, :], in_=bk[:, ci * LT:ci * LT + LT],
                                func=AF.Silu, bias=cbs[:, cp + ci:cp + ci + 1]), ss)
                        P.evac(bn, ss)
                        P.wait("tensor", ss, P.val(ss))
                        P.wait("tensor", vs, P.val(vs))
                        srcs = [xchTs2[:, 0, 0:128], xchTs2[:, 1, 0:128],
                                xchTs2[:, 0, 128:256], xchTs2[:, 1, 128:256]]
                        for i, sap in enumerate(srcs):
                            P.op("tensor", lambda e, sap=sap, i=i: e.transpose(
                                out=pT[:, i * 128:(i + 1) * 128], in_=sap, identity=identbs[:]),
                                ts if i == 3 else None)
                        P.wait("vector", ts, P.val(ts))
                        P.op("vector", lambda e, cp=cp: e.tensor_copy(
                            out=xtok[:, 0:2, cp * 128:(cp + 2) * 128].rearrange(
                                "p j (c x) -> p j c x", c=2),
                            in_=pT[:, 0:512].rearrange("p (j c x) -> p j c x", j=2, c=2)), vs)
                    else:
                        for ci in range(2):
                            P.op("scalar", lambda e, ci=ci, cp=cp, bk=bk: e.activation(
                                out=btct[:, ci, 0:LT], in_=bk[:, ci * LT:ci * LT + LT],
                                func=AF.Silu, bias=cbs[:, cp + ci:cp + ci + 1]), ss)
                        P.evac(bn, ss)
                P.wait("tensor", ss, P.val(ss))
                transpose_batch(
                    [btct[:, 0, j * 128:j * 128 + 128] for j in range(LT // 128)],
                    btok[:, 0:LT // 128, :])
                dt_pipeline()
                for c in range(LT // CH):
                    chunk_ssd(c, c * CH, full)
                if full:
                    outproj_tile(it)

        attn_pass(xp, full=False)
        P.op("vector", lambda e: e.tensor_scalar(
            out=prev[:], in0=prev[:], scalar1=sels[:, 0:1], scalar2=None, op0=ALU.mult), vs)
        attn_pass(xo, full=True)

        for eng in ENGS:
            for s in (ld, gq, sq, ts, vs, ss):
                P.wait(eng, s)
        esA.close()

        iota8s = S(nc.sbuf_tensor("iota8s", [128, 8], F32))
        maskcs = S(nc.sbuf_tensor("maskcs", [8, 1], F32))
        cnt8s = S(nc.sbuf_tensor("cnt8s", [8, E], F32))
        bases = S(nc.sbuf_tensor("bases", [1, E], F32))
        basen = S(nc.sbuf_tensor("basen", [128, E], F32))
        runA = S(nc.sbuf_tensor("runA", [1, E], F32))
        ysc = [S(nc.sbuf_tensor(f"ysc{i}", [128, DM], F32)) for i in range(2)]
        ind8 = S(nc.sbuf_tensor("ind8", [128, 8], F32))
        recs = [S(nc.sbuf_tensor(f"recs{i}", [128, 2], F32)) for i in range(2)]
        tu32 = [S(nc.sbuf_tensor(f"tu32{i}", [128, 1], U32)) for i in range(2)]

        pA2 = S(nc.psum_tensor("pA2", [128, 512], F32))
        pB2 = S(nc.psum_tensor("pB2", [128, 512], F32))
        pC2 = S(nc.psum_tensor("pC2", [128, 512], F32))
        pD2 = S(nc.psum_tensor("pD2", [128, 512], F32))
        pRt = S(nc.psum_tensor("pRt", [128, E], F32))
        pCA = S(nc.psum_tensor("pCA", [1, E], F32))

        esB1 = ExitStack()
        B1 = esB1.enter_context
        lstTs = B1(nc.sbuf_tensor("lstTs", [128, 128], F32))
        clocbs = B1(nc.sbuf_tensor("clocbs", [128, E], F32))
        wgts = B1(nc.sbuf_tensor("wgts", [128, 8, E], BF16))
        xnlt = B1(nc.sbuf_tensor("xnlt", [128, 4, DM], F32))
        l2bf = [B1(nc.sbuf_tensor(f"l2bf{i}", [128, DM], BF16)) for i in range(2)]
        l2T = B1(nc.sbuf_tensor("l2T", [128, 8, 512], BF16))
        gsc = B1(nc.sbuf_tensor("gsc", [128, E], F32))
        gtmp = B1(nc.sbuf_tensor("gtmp", [128, E], F32))
        gm8 = B1(nc.sbuf_tensor("gm8", [128, 8], F32))
        ggs = B1(nc.sbuf_tensor("ggs", [128, 8], F32))
        gkeep = B1(nc.sbuf_tensor("gkeep", [128, 8], F32))
        amask = B1(nc.sbuf_tensor("amask", [128, E], F32))
        dg = B1(nc.sbuf_tensor("dg", [128, E], F32))
        dwork = B1(nc.sbuf_tensor("dwork", [128, E], F32))
        rmat = B1(nc.sbuf_tensor("rmat", [128, E], F32))
        dcol = [B1(nc.sbuf_tensor(f"dcol{i}", [128, 4], F32)) for i in range(2)]
        du32 = [B1(nc.sbuf_tensor(f"du32{i}", [128, 4], U32)) for i in range(2)]
        rec2 = [B1(nc.sbuf_tensor(f"rec2{i}", [128, 8], F32)) for i in range(2)]
        swb = B1(nc.sbuf_tensor("swb", [128, 8, 3 * DM], BF16))
        shT = B1(nc.sbuf_tensor("shT", [128, 8, 512], BF16))
        pT2 = B1(nc.psum_tensor("pT2", [128, 512], BF16))

        for name, dst in [("lstT", lstTs), ("clocb", clocbs),
                          ("iota8", iota8s), ("maskc", maskcs)]:
            P.op("sync", lambda e, a=ap[name], d=dst: e.dma_start(out=d[:], in_=a[:, :]), ld, 16)
        P.op("gpsimd", lambda e: e.dma_start(
            out=wgts[:], in_=wgt[:, :].rearrange("(k p) n -> p k n", p=128)), gq, 16)
        P.op("gpsimd", lambda e: e.dma_start(
            out=swb[:, :, 0:DM], in_=sw1[:, :].rearrange("(k p) n -> p k n", p=128)), gq, 16)
        P.op("gpsimd", lambda e: e.dma_start(
            out=swb[:, :, DM:2 * DM], in_=sw3[:, :].rearrange("(k p) n -> p k n", p=128)), gq, 16)
        P.op("gpsimd", lambda e: e.dma_start(
            out=swb[:, :, 2 * DM:3 * DM], in_=sw2[:, :].rearrange("(k p) n -> p k n", p=128)), gq, 16)
        P.op("vector", lambda e: e.memset(runA[:], 0.0), vs)
        P.wait("tensor", ld, P.val(ld))
        P.wait("tensor", gq, P.val(gq))
        P.wait("tensor", vs, P.val(vs))
        P.wait("vector", ld, P.val(ld))
        P.wait("vector", gq, P.val(gq))
        P.wait("gpsimd", sq, zval)

        def ln2_gate_route(jg):
            par = jg % 2
            jl = jg % 4
            P.op("sync", lambda e, jg=jg: e.dma_start(
                out=xnlt[:, jl, :], in_=xnw[jg * 128:(jg + 1) * 128, :]), ld, 16)
            P.wait("vector", ld, P.val(ld))
            P.wait("vector", gq, P.val(gq))
            P.op("vector", lambda e, jl=jl: e.tensor_reduce(
                out=cols[:, 0:1], in_=xnlt[:, jl, :], axis=AX, op=ALU.add), vs)
            P.op("vector", lambda e: e.tensor_scalar_mul(out=cols[:, 0:1], in0=cols[:, 0:1], scalar1=1.0 / DM), vs)
            P.op("vector", lambda e, jl=jl: e.tensor_scalar(
                out=hwork[:], in0=xnlt[:, jl, :], scalar1=cols[:, 0:1], scalar2=None, op0=ALU.subtract), vs)
            P.wait("scalar", vs, P.val(vs))
            P.op("scalar", lambda e: e.activation(
                out=arena[:, 0:DM], in_=hwork[:], func=AF.Square, accum_out=cols[:, 1:2]), ss)
            P.op("scalar", lambda e: e.activation(
                out=cols[:, 2:3], in_=cols[:, 1:2], func=AF.Ln, scale=1.0 / DM, bias=epsc[:, 0:1]), ss)
            P.op("scalar", lambda e: e.activation(
                out=cols[:, 2:3], in_=cols[:, 2:3], func=AF.Exp, scale=-0.5), ss)
            P.wait("vector", ss, P.val(ss))
            P.op("vector", lambda e, par=par: e.tensor_scalar(
                out=l2bf[par][:], in0=hwork[:], scalar1=cols[:, 2:3], scalar2=None, op0=ALU.mult), vs)
            P.wait("tensor", vs, P.val(vs))
            for k0 in (0, 4):
                P.guard("pT2")
                for i in range(4):
                    P.op("tensor", lambda e, k0=k0, i=i, par=par: e.transpose(
                        out=pT2[:, i * 128:(i + 1) * 128],
                        in_=l2bf[par][:, (k0 + i) * 128:(k0 + i + 1) * 128],
                        identity=identbs[:]), ts if i == 3 else None)
                P.wait("vector", ts, P.val(ts))
                P.op("vector", lambda e, k0=k0, jl=jl: e.tensor_copy(
                    out=l2T[:, k0:k0 + 4, jl * 128:(jl + 1) * 128],
                    in_=pT2[:, 0:512].rearrange("p (k x) -> p k x", k=4)), vs)
                P.evac("pT2", vs)
            P.wait("tensor", vs, P.val(vs))
            P.guard("pRt")
            for k in range(8):
                P.op("tensor", lambda e, k=k, jl=jl: e.matmul(
                    out=pRt[:, :], lhsT=l2T[:, k, jl * 128:(jl + 1) * 128], rhs=wgts[:, k, :],
                    start=(k == 0), stop=(k == 7)), ts if k == 7 else None)
            P.wait("vector", ts, P.val(ts))
            P.op("vector", lambda e: e.tensor_reduce(
                out=cols[:, 0:1], in_=pRt[:, :], axis=AX, op=ALU.max, negate=True), vs)
            P.wait("scalar", vs, P.val(vs))
            P.op("scalar", lambda e: e.activation(out=gsc[:], in_=pRt[:, :], func=AF.Exp, bias=cols[:, 0:1]), ss)
            P.evac("pRt", ss)
            P.wait("vector", ss, P.val(ss))
            P.op("vector", lambda e: e.tensor_reduce(out=cols[:, 1:2], in_=gsc[:], axis=AX, op=ALU.add), vs)
            P.op("vector", lambda e: e.reciprocal(out=cols[:, 2:3], in_=cols[:, 1:2]), vs)
            P.op("vector", lambda e: e.tensor_scalar(
                out=gsc[:], in0=gsc[:], scalar1=cols[:, 2:3], scalar2=None, op0=ALU.mult), vs)
            P.op("vector", lambda e: e.tensor_reduce(
                out=ggs[:].rearrange("p g -> p g ()"),
                in_=gsc[:].rearrange("p (g x) -> p g x", g=8), axis=AX, op=ALU.max), vs)
            P.op("vector", lambda e: e.max(out=gm8[:], in_=ggs[:]), vs)
            P.op("vector", lambda e: e.tensor_scalar(
                out=gkeep[:], in0=ggs[:], scalar1=gm8[:, 3:4], scalar2=None, op0=ALU.is_ge), vs)
            P.op("vector", lambda e: e.tensor_tensor(
                out=gtmp[:].rearrange("p (g x) -> p g x", g=8),
                in0=gsc[:].rearrange("p (g x) -> p g x", g=8),
                in1=gkeep[:].rearrange("p g -> p g ()").broadcast_to((128, 8, 4)), op=ALU.mult), vs)
            P.op("vector", lambda e: e.tensor_scalar(
                out=gkeep[:], in0=gkeep[:], scalar1=0.5, scalar2=None, op0=ALU.is_lt), vs)
            P.op("vector", lambda e: e.scalar_tensor_tensor(
                out=gtmp[:].rearrange("p (g x) -> p g x", g=8),
                in0=gkeep[:].rearrange("p g -> p g ()").broadcast_to((128, 8, 4)),
                scalar=-1e30, in1=gtmp[:].rearrange("p (g x) -> p g x", g=8),
                op0=ALU.mult, op1=ALU.add), vs)
            P.op("vector", lambda e: e.max(out=gm8[:], in_=gtmp[:]), vs)
            P.op("vector", lambda e: e.tensor_scalar(
                out=amask[:], in0=gtmp[:], scalar1=gm8[:, 3:4], scalar2=None, op0=ALU.is_ge), vs)
            P.op("vector", lambda e: e.tensor_tensor(out=rmat[:], in0=amask[:], in1=gsc[:], op=ALU.mult), vs)
            P.op("vector", lambda e: e.tensor_scalar(
                out=dwork[:], in0=rmat[:], scalar1=1.0, scalar2=0.0, op0=ALU.mult, op1=ALU.add,
                accum_out=cols[:, 3:4]), vs)
            P.op("vector", lambda e: e.tensor_scalar_add(out=cols[:, 3:4], in0=cols[:, 3:4], scalar1=1e-9), vs)
            P.op("vector", lambda e: e.reciprocal(out=cols[:, 4:5], in_=cols[:, 3:4]), vs)
            P.op("vector", lambda e: e.tensor_scalar(
                out=rmat[:], in0=rmat[:], scalar1=cols[:, 4:5], scalar2=None, op0=ALU.mult), vs)
            P.wait("tensor", vs, P.val(vs))
            P.guard("pRt")
            P.op("tensor", lambda e: e.matmul(out=pRt[:, :], lhsT=lstTs[:, :], rhs=amask[:],
                                              start=True, stop=False), None)
            P.op("tensor", lambda e: e.matmul(out=pRt[:, :], lhsT=onesrs[:, :],
                                              rhs=runA[:, :], start=False, stop=True), ts)
            P.guard("pCA")
            P.op("tensor", lambda e: e.matmul(
                out=pCA[:, :], lhsT=onecs[:, 0:1], rhs=amask[:],
                start=True, stop=True), ts)
            P.wait("vector", ts, P.val(ts))
            P.op("vector", lambda e: e.tensor_scalar(
                out=dwork[:], in0=pRt[:, :], scalar1=float(C_LOC), scalar2=None, op0=ALU.is_ge), vs)
            P.op("vector", lambda e: e.tensor_tensor(out=dg[:], in0=pRt[:, :], in1=clocbs[:], op=ALU.add), vs)
            P.evac("pRt", vs)
            P.op("vector", lambda e: e.tensor_tensor(out=runA[:], in0=runA[:], in1=pCA[:, :], op=ALU.add), vs)
            P.evac("pCA", vs)
            P.op("vector", lambda e: e.scalar_tensor_tensor(
                out=dg[:], in0=dwork[:], scalar=DUMP, in1=dg[:], op0=ALU.mult, op1=ALU.add), vs)
            for kk in range(4):
                P.op("vector", lambda e, kk=kk, par=par: e.scalar_tensor_tensor(
                    out=dwork[:], in0=gtmp[:], scalar=gm8[:, kk:kk + 1], in1=dg[:],
                    op0=ALU.is_equal, op1=ALU.mult, accum_out=dcol[par][:, kk:kk + 1]), vs)
                P.op("vector", lambda e, kk=kk, par=par: e.scalar_tensor_tensor(
                    out=dwork[:], in0=gtmp[:], scalar=gm8[:, kk:kk + 1], in1=rmat[:],
                    op0=ALU.is_equal, op1=ALU.mult, accum_out=rec2[par][:, 2 * kk:2 * kk + 1]), vs)
                P.op("vector", lambda e, kk=kk, par=par, jg=jg: e.tensor_scalar_add(
                    out=rec2[par][:, 2 * kk + 1:2 * kk + 2], in0=iota8s[:, 0:1],
                    scalar1=float(jg * 128)), vs)
            P.op("vector", lambda e, par=par: e.tensor_scalar_min(
                out=dcol[par][:], in0=dcol[par][:], scalar1=float(EB_ROWS - 1)), vs)
            P.op("vector", lambda e, par=par: e.tensor_copy(out=du32[par][:], in_=dcol[par][:]), vs)
            P.wait("gpsimd", vs, P.val(vs))
            for kk in range(4):
                P.op("gpsimd", lambda e, kk=kk, par=par: e.indirect_dma_start(
                    out=ebuf.ap(), out_offset=IndirectOffsetOnAxis(du32[par][:, kk:kk + 1], 0),
                    in_=l2bf[par][:, :], in_offset=None), gq, 16)
                P.op("gpsimd", lambda e, kk=kk, par=par: e.indirect_dma_start(
                    out=rec.ap(), out_offset=IndirectOffsetOnAxis(du32[par][:, kk:kk + 1], 0),
                    in_=rec2[par][:, 2 * kk:2 * kk + 2], in_offset=None), gq, 16)

        def shared_lt(lt):
            for ic in range(8):
                P.wait("tensor", vs, P.val(vs))
                P.guard("pA2")
                P.guard("pB2")
                for k in range(8):
                    P.op("tensor", lambda e, ic=ic, k=k: e.matmul(
                        out=pA2[:, 0:512], lhsT=swb[:, k, ic * 128:ic * 128 + 128],
                        rhs=l2T[:, k, :], start=(k == 0), stop=(k == 7)), None)
                for k in range(8):
                    P.op("tensor", lambda e, ic=ic, k=k: e.matmul(
                        out=pB2[:, 0:512], lhsT=swb[:, k, DM + ic * 128:DM + ic * 128 + 128],
                        rhs=l2T[:, k, :], start=(k == 0), stop=(k == 7)),
                        ts if k == 7 else None)
                P.wait("scalar", ts, P.val(ts))
                P.op("scalar", lambda e, ic=ic: e.activation(out=arena[:, 0:512], in_=pA2[:, 0:512], func=AF.Silu), ss)
                P.evac("pA2", ss)
                P.wait("vector", ss, P.val(ss))
                P.op("vector", lambda e, ic=ic: e.tensor_tensor(
                    out=shT[:, ic, :], in0=arena[:, 0:512], in1=pB2[:, 0:512], op=ALU.mult), vs)
                P.evac("pB2", vs)
            P.wait("tensor", vs, P.val(vs))
            for j2 in range(4):
                for n in range(2):
                    P.guard("pC2")
                    for k in range(8):
                        P.op("tensor", lambda e, j2=j2, n=n, k=k: e.matmul(
                            out=pC2[:, 0:512], lhsT=shT[:, k, j2 * 128:j2 * 128 + 128],
                            rhs=swb[:, k, 2 * DM + n * 512:2 * DM + (n + 1) * 512],
                            start=(k == 0), stop=(k == 7)), ts if k == 7 else None)
                    P.wait("vector", ts, P.val(ts))
                    P.wait("vector", ost, P.val(ost))
                    P.op("vector", lambda e, j2=j2, n=n: e.tensor_tensor(
                        out=ysc[0][:, n * 512:(n + 1) * 512], in0=pC2[:, 0:512],
                        in1=xnlt[:, j2, n * 512:(n + 1) * 512], op=ALU.add), vs)
                    P.evac("pC2", vs)
                P.wait("sync", vs, P.val(vs))
                P.op("sync", lambda e, lt=lt, j2=j2: e.dma_start(
                    out=ap["out"][lt * 512 + j2 * 128:lt * 512 + (j2 + 1) * 128, :], in_=ysc[0][:]), ost, 16)

        for lt in range(8):
            for j in range(4):
                ln2_gate_route(lt * 4 + j)
            if lt == 7:
                P.wait("sync", vs, P.val(vs))
                P.op("sync", lambda e: e.dma_start(out=cntin.ap()[:, :], in_=runA[:]), ld, 16)
                P.wait("gpsimd", ld, P.val(ld))
                P.op("gpsimd", lambda e: e.collective_compute(
                    "AllGather", ALU.bypass,
                    replica_groups=[[0, 1, 2, 3, 4, 5, 6, 7]],
                    ins=[cntin.ap()[:, :].opt()], outs=[cnt8o.ap()[:, :].opt()]), "ccs", 1)
                cdone = P.val("ccs")
            shared_lt(lt)

        P.wait("sync", "ccs", cdone)
        P.op("sync", lambda e: e.dma_start(out=cnt8s[:], in_=cnt8o.ap()[:, :]), ld, 16)
        P.wait("tensor", ld, P.val(ld))
        P.guard("pCA")
        P.op("tensor", lambda e: e.matmul(out=pCA[:, :], lhsT=maskcs[:, 0:1], rhs=cnt8s[:, :],
                                          start=True, stop=True), ts)
        P.wait("vector", ts, P.val(ts))
        P.op("vector", lambda e: e.tensor_copy(out=bases[:], in_=pCA[:, :]), vs)
        P.evac("pCA", vs)
        P.wait("tensor", vs, P.val(vs))
        P.guard("pRt")
        P.op("tensor", lambda e: e.matmul(out=pRt[:, :], lhsT=onesrs[0:1, :], rhs=bases[0:1, :],
                                          start=True, stop=True), ts)
        P.wait("vector", ts, P.val(ts))
        P.op("vector", lambda e: e.tensor_scalar(
            out=basen[:], in0=pRt[:, :], scalar1=-1.0, scalar2=float(CAP),
            op0=ALU.mult, op1=ALU.add), vs)
        P.evac("pRt", vs)

        for eng in ENGS:
            for s in (ld, gq, sq, ts, vs, ss):
                P.wait(eng, s)
        esB1.close()

        w1b = [S(nc.sbuf_tensor(f"w1b{i}", [128, 8, DM], BF16)) for i in range(2)]
        w3b = [S(nc.sbuf_tensor(f"w3b{i}", [128, 8, DM], BF16)) for i in range(2)]
        w2b = [S(nc.sbuf_tensor(f"w2b{i}", [128, 8, DM], BF16)) for i in range(2)]
        xerows = [S(nc.sbuf_tensor(f"xerows{i}", [128, 8, DM], BF16)) for i in range(2)]
        xeT = S(nc.sbuf_tensor("xeT", [128, 8, C_LOC], BF16))
        hT = S(nc.sbuf_tensor("hT", [128, 8, C_LOC], BF16))
        pTa = S(nc.psum_tensor("pTa", [128, 1024], BF16))
        pTb = S(nc.psum_tensor("pTb", [128, 1024], BF16))

        ldv, sqv, gqv, tsdone = {}, {}, {}, {}
        scv = [0, 0]

        def issue_loads(e2):
            par = e2 % 2
            if e2 >= 2:
                g = tsdone[e2 - 2]
                P.wait("sync", ts, g)
                P.wait("scalar", ts, g)
                P.wait("gpsimd", ts, g)
            P.op("sync", lambda e, e2=e2, par=par: e.dma_start(
                out=w1b[par][:], in_=w1_i[e2 * 128:(e2 + 1) * 128, :].rearrange(
                    "p (k n) -> p k n", k=8)), ld, 16)
            ldv[e2] = P.val(ld)
            P.op("scalar", lambda e, e2=e2, par=par: e.dma_start(
                out=w3b[par][:], in_=w3_i[e2 * 128:(e2 + 1) * 128, :].rearrange(
                    "p (k n) -> p k n", k=8)), sq, 16)
            P.op("scalar", lambda e, e2=e2, par=par: e.dma_start(
                out=xerows[par][:], in_=ebuf[e2 * C_LOC:(e2 + 1) * C_LOC, :].rearrange(
                    "(s p) n -> p s n", p=128)), sq, 16)
            sqv[e2] = P.val(sq)
            P.op("gpsimd", lambda e, e2=e2, par=par: e.dma_start(
                out=w2b[par][:], in_=w2_i[e2 * 128:(e2 + 1) * 128, :].rearrange(
                    "p (k n) -> p k n", k=8)), gq, 16)
            gqv[e2] = P.val(gq)

        issue_loads(0)
        issue_loads(1)
        for eix in range(E):
            par = eix % 2
            w1c, w3c, w2c = w1b[par], w3b[par], w2b[par]
            P.op("vector", lambda e, eix=eix: e.tensor_scalar(
                out=ind8[:], in0=iota8s[:], scalar1=basen[:, eix:eix + 1], scalar2=None,
                op0=ALU.is_lt), vs)
            P.wait("tensor", ld, ldv[eix])
            P.wait("tensor", sq, sqv[eix])
            P.wait("tensor", gq, gqv[eix])
            for kc in range(8):
                bank = pTa if kc % 2 == 0 else pTb
                P.guard("pTa" if kc % 2 == 0 else "pTb")
                for s in range(8):
                    P.op("tensor", lambda e, kc=kc, s=s, par=par, bank=bank: e.transpose(
                        out=bank[:, s * 128:(s + 1) * 128],
                        in_=xerows[par][:, s, kc * 128:(kc + 1) * 128],
                        identity=identbs[:]), ts if s == 7 else None)
                P.wait("vector", ts, P.val(ts))
                P.op("vector", lambda e, kc=kc, bank=bank: e.tensor_copy(
                    out=xeT[:, kc, :], in_=bank[:, :]), vs)
                P.evac("pTa" if kc % 2 == 0 else "pTb", vs)
            P.wait("tensor", vs, P.val(vs))
            for ic in range(8):
                for nn in range(2):
                    g = (ic * 2 + nn) % 2
                    b1, n1, b2, n2 = (pA2, "pA2", pB2, "pB2") if g == 0 else (pC2, "pC2", pD2, "pD2")
                    ar = arena[:, g * 512:(g + 1) * 512]
                    P.guard(n1)
                    P.guard(n2)
                    for k in range(8):
                        P.op("tensor", lambda e, ic=ic, nn=nn, k=k, w1c=w1c, b1=b1: e.matmul(
                            out=b1[:, 0:512], lhsT=w1c[:, k, ic * 128:ic * 128 + 128],
                            rhs=xeT[:, k, nn * 512:(nn + 1) * 512], start=(k == 0), stop=(k == 7)), None)
                    for k in range(8):
                        P.op("tensor", lambda e, ic=ic, nn=nn, k=k, w3c=w3c, b2=b2: e.matmul(
                            out=b2[:, 0:512], lhsT=w3c[:, k, ic * 128:ic * 128 + 128],
                            rhs=xeT[:, k, nn * 512:(nn + 1) * 512], start=(k == 0), stop=(k == 7)),
                            ts if k == 7 else None)
                    P.wait("scalar", ts, P.val(ts))
                    P.op("scalar", lambda e, b1=b1, ar=ar: e.activation(out=ar, in_=b1[:, 0:512], func=AF.Silu), ss)
                    P.evac(n1, ss)
                    P.wait("vector", ss, P.val(ss))
                    P.op("vector", lambda e, ic=ic, nn=nn, b2=b2, ar=ar: e.tensor_tensor(
                        out=hT[:, ic, nn * 512:(nn + 1) * 512], in0=ar,
                        in1=b2[:, 0:512], op=ALU.mult), vs)
                    P.evac(n2, vs)
            P.wait("tensor", vs, P.val(vs))
            for rt in range(8):
                g = rt % 2
                b1, n1, b2, n2 = (pA2, "pA2", pB2, "pB2") if g == 0 else (pC2, "pC2", pD2, "pD2")
                P.guard(n1)
                P.guard(n2)
                for nn in range(2):
                    for k in range(8):
                        P.op("tensor", lambda e, rt=rt, nn=nn, k=k, w2c=w2c, b1=b1, b2=b2: e.matmul(
                            out=(b1 if nn == 0 else b2)[:, 0:512],
                            lhsT=hT[:, k, rt * 128:rt * 128 + 128],
                            rhs=w2c[:, k, nn * 512:(nn + 1) * 512],
                            start=(k == 0), stop=(k == 7)), ts if k == 7 else None)
                P.wait("sync", vs, P.val(vs))
                P.op("sync", lambda e, eix=eix, rt=rt, g=g: e.dma_start(
                    out=recs[g][:], in_=rec[eix * C_LOC + rt * 128:eix * C_LOC + (rt + 1) * 128, :]), ld, 16)
                P.wait("vector", gq, scv[g])
                P.wait("vector", ts, P.val(ts))
                P.wait("vector", ld, P.val(ld))
                P.op("vector", lambda e, rt=rt, g=g, b1=b1: e.tensor_scalar(
                    out=ysc[g][:, 0:512], in0=b1[:, 0:512], scalar1=recs[g][:, 0:1],
                    scalar2=ind8[:, rt:rt + 1], op0=ALU.mult, op1=ALU.mult), vs)
                P.op("vector", lambda e, rt=rt, g=g, b2=b2: e.tensor_scalar(
                    out=ysc[g][:, 512:1024], in0=b2[:, 0:512], scalar1=recs[g][:, 0:1],
                    scalar2=ind8[:, rt:rt + 1], op0=ALU.mult, op1=ALU.mult), vs)
                P.evac(n1, vs)
                P.evac(n2, vs)
                P.op("vector", lambda e, g=g: e.tensor_copy(out=tu32[g][:], in_=recs[g][:, 1:2]), vs)
                P.wait("gpsimd", vs, P.val(vs))
                P.op("gpsimd", lambda e, g=g: e.indirect_dma_start(
                    out=ap["out"], out_offset=IndirectOffsetOnAxis(tu32[g][:, 0:1], 0),
                    in_=ysc[g][:, :], in_offset=None, compute_op=ALU.add), gq, 16)
                scv[g] = P.val(gq)
            tsdone[eix] = P.val(ts)
            if eix + 2 < E:
                issue_loads(eix + 2)

        P.wait("sync", gq, P.val(gq))
        P.wait("sync", ld, P.val(ld))
        P.wait("sync", ost, P.val(ost))

        block = S(nc.Block())
        P.run(block)
    return nc


def host_inputs(inputs):
    x = np.ascontiguousarray(inputs["L0_attn_hidden_states"].reshape(4 * 8192, DM))
    W_in = inputs["L0_attn_W_in"]
    winp = np.zeros((DM, DIP), np.float32)
    winp[:, :4384] = W_in
    conv_w = inputs["L0_attn_conv_w"]
    convd = np.zeros((4 * NCT * 128, 128), np.float32)
    for tap in range(4):
        for ct in range(NCT):
            blk = convd[(tap * NCT + ct) * 128:(tap * NCT + ct + 1) * 128]
            np.fill_diagonal(blk, conv_w[ct * 128:(ct + 1) * 128, tap])
    convb = np.ascontiguousarray(inputs["L0_attn_conv_b"].reshape(NCT, 128).T)
    dtb = inputs["L0_attn_dt_bias"].reshape(32, 1)
    alog = inputs["L0_attn_A_log"].reshape(32, 1)
    dbc = np.broadcast_to(np.repeat(inputs["L0_attn_D"], 64)[None, :], (128, DI)).copy()
    rmsbc = np.broadcast_to(inputs["L0_attn_rms_w"][None, :], (128, DI)).copy()
    lngbc = np.broadcast_to(inputs["L0_attn_ln_g"][None, :], (128, DM)).copy()
    lnbbc = np.broadcast_to(inputs["L0_attn_ln_b"][None, :], (128, DM)).copy()
    wout = inputs["L0_attn_W_out"]
    wgt = np.ascontiguousarray(inputs["L0_ffn_wg"].T)
    diagm = np.zeros((128, 8 * 128), ml_dtypes.bfloat16)
    for h in range(8):
        np.fill_diagonal(diagm[:, h * 128:(h + 1) * 128], ml_dtypes.bfloat16(1.0))
    bigm = np.where(np.arange(128)[:, None] <= np.arange(128)[None, :], 0.0, -30000.0).astype(np.float32)
    ones128 = np.ones((128, 128), np.float32)
    identf = np.eye(128, dtype=np.float32)
    identb = np.eye(128, dtype=ml_dtypes.bfloat16)
    onesr = np.ones((1, 128), np.float32)
    onescol = np.ones((128, 1), np.float32)
    oh127 = np.zeros((128, 1), np.float32)
    oh127[127, 0] = 1.0
    w1 = np.ascontiguousarray(inputs["L0_ffn_w1"].reshape(E * DM, DM))
    w3 = np.ascontiguousarray(inputs["L0_ffn_w3"].reshape(E * DM, DM))
    w2 = np.ascontiguousarray(inputs["L0_ffn_w2"].reshape(E * DM, DM))
    lstT = np.where(np.arange(128)[:, None] < np.arange(128)[None, :], 1.0, 0.0).astype(np.float32)
    clocb = np.broadcast_to((np.arange(E) * C_LOC).astype(np.float32)[None, :], (128, E)).copy()
    iota8 = (np.arange(128, dtype=np.float32)[:, None]
             + 128.0 * np.arange(8, dtype=np.float32)[None, :]).copy()
    maps = []
    for c in range(NCORES):
        b, half = c // 2, c % 2
        base = b * 8192 + half * NTOK
        xo = np.zeros((NTOK + 3, DM), np.float32)
        xo[3:] = x[base:base + NTOK]
        if half == 1:
            xo[:3] = x[base - 3:base]
        xpred = np.zeros((NTOK + 3, DM), np.float32)
        if half == 1:
            xpred[3:] = x[b * 8192:b * 8192 + NTOK]
        selc = np.full((128, 1), float(half), np.float32)
        maskc = (np.arange(8) < c).astype(np.float32).reshape(8, 1)
        maps.append({
            "xo": xo, "xp": xpred, "win": winp, "convd": convd, "convb": convb,
            "dtb": dtb, "alog": alog, "dbc": dbc, "rmsbc": rmsbc, "lngbc": lngbc,
            "lnbbc": lnbbc, "wout": wout, "selc": selc, "diagm": diagm,
            "bigm": bigm, "ones128": ones128, "identf": identf, "identb": identb,
            "onesr": onesr, "onescol": onescol, "oh127": oh127,
            "wgt": wgt, "w1": w1, "w3": w3, "w2": w2,
            "sw1": inputs["L0_ffn_sw1"], "sw3": inputs["L0_ffn_sw3"], "sw2": inputs["L0_ffn_sw2"],
            "lstT": lstT, "clocb": clocb, "iota8": iota8, "maskc": maskc,
        })
    return maps


def kernel(**inputs):
    maps = host_inputs(inputs)
    nc = build()
    res = run_bass_kernel_spmd(nc, maps, core_ids=list(range(NCORES)))
    out = np.zeros((4, 8192, DM), np.float32)
    for c in range(NCORES):
        b, half = c // 2, c % 2
        out[b, half * NTOK:(half + 1) * NTOK] = np.asarray(res.results[c]["out"])
    return out


# version 12 (final)
# speedup vs baseline: 1.1385x; 1.1385x over previous
import numpy as np
import ml_dtypes
import concourse.bass as bass
import concourse.mybir as mybir
from concourse.bass import IndirectOffsetOnAxis
from concourse.bass_utils import run_bass_kernel_spmd
from contextlib import ExitStack

F32 = mybir.dt.float32
BF16 = mybir.dt.bfloat16
U32 = mybir.dt.uint32
AF = mybir.ActivationFunctionType
ALU = mybir.AluOpType
AX = mybir.AxisListType.X

NCORES = 8
NTOK = 4096
LT = 256
NLT = NTOK // LT
CH = 128
DM = 1024
DI = 2048
DST = 128
NH = 32
CONVD = 2304
NCT = 18
DIP = 4480
WXW = 2336
EPS = 1e-5
E = 32
CAP = 5120
C_LOC = 1024
EB_ROWS = E * C_LOC + 128
DUMP = 10_000_000.0

ENGS = ("sync", "gpsimd", "tensor", "vector", "scalar")
SERIAL_ENGS = {"vector": "vs", "scalar": "ss"}

DMA_OWNER = {"ld": "sync", "gq": "gpsimd", "sq": "scalar"}


class Prog:
    def __init__(self, nc, es):
        self.nc = nc
        self.ops = {e: [] for e in ENGS}
        self.cnt = {}
        self.sems = {}
        self.es = es
        self.last_use = {}
        self.last_throttle = {}
        self.bankdep = {}
        self.last_op = {}

    def sem(self, name):
        if name not in self.sems:
            self.sems[name] = self.es.enter_context(self.nc.semaphore(name))
            self.cnt[name] = 0
        return self.sems[name]

    def val(self, name):
        return self.cnt.get(name, 0)

    def op(self, eng, fn, sem=None, n=1):
        if eng in SERIAL_ENGS:
            if sem is None:
                sem = SERIAL_ENGS[eng]
            le = self.last_op.get(eng)
            if le is not None:
                self.ops[eng].append(lambda E, s=le[0], v=le[1]: E.wait_ge(s, v))
            s = self.sem(sem)
            self.cnt[sem] += n
            self.ops[eng].append(lambda E, fn=fn, s=s, n=n: fn(E).then_inc(s, n))
            self.last_op[eng] = (s, self.cnt[sem])
            return self.cnt[sem]
        if sem is None:
            self.ops[eng].append(fn)
            return None
        s = self.sem(sem)
        if sem in DMA_OWNER:
            lu = self.last_use.get(sem, 0)
            if lu > self.last_throttle.get(sem, 0):
                owner = DMA_OWNER[sem]
                self.ops[owner].append(lambda E, s=s, v=lu: E.wait_ge(s, v))
                self.last_throttle[sem] = lu
        self.cnt[sem] += n
        self.ops[eng].append(lambda E, fn=fn, s=s, n=n: fn(E).then_inc(s, n))
        return self.cnt[sem]

    def guard(self, bank):
        for s_, v_ in self.bankdep.get(bank, {}).items():
            self.wait("tensor", s_, v_)

    def evac(self, bank, sem):
        self.bankdep.setdefault(bank, {})[sem] = self.val(sem)

    def wait(self, eng, sem, v=None):
        if v is None:
            v = self.val(sem)
        if v <= 0:
            return
        if sem in DMA_OWNER:
            self.last_use[sem] = max(self.last_use.get(sem, 0), v)
        s = self.sem(sem)
        self.ops[eng].append(lambda E, s=s, v=v: E.wait_ge(s, v))

    def run(self, block):
        P = self

        @block.sync
        def _(e):
            for f in P.ops["sync"]:
                f(e)

        @block.gpsimd
        def _(e):
            for f in P.ops["gpsimd"]:
                f(e)

        @block.tensor
        def _(e):
            for f in P.ops["tensor"]:
                f(e)

        @block.vector
        def _(e):
            for f in P.ops["vector"]:
                f(e)

        @block.scalar
        def _(e):
            for f in P.ops["scalar"]:
                f(e)


def build():
    nc = bass.Bass("TRN2", target_bir_lowering=False, debug=False)
    ap = {}

    def din(name, shape, dt=F32):
        ap[name] = nc.dram_tensor(name, shape, dt, kind="ExternalInput").ap()
        return ap[name]

    def dout(name, shape, dt=F32):
        ap[name] = nc.dram_tensor(name, shape, dt, kind="ExternalOutput").ap()
        return ap[name]

    xo = din("xo", [NTOK + 3, DM])
    xp = din("xp", [NTOK + 3, DM])
    win = din("win", [DM, DIP])
    convd = din("convd", [4 * NCT * 128, 128])
    din("convb", [128, NCT])
    din("dtb", [32, 1])
    din("alog", [32, 1])
    din("dbc", [128, DI])
    din("rmsbc", [128, DI])
    din("lngbc", [128, DM])
    din("lnbbc", [128, DM])
    wout = din("wout", [DI, DM])
    din("selc", [128, 1])
    din("diagm", [128, 8 * 128], BF16)
    din("bigm", [128, 128])
    din("ones128", [128, 128])
    din("identf", [128, 128])
    din("identb", [128, 128], BF16)
    din("onesr", [1, 128])
    din("onescol", [128, 1])
    din("oh127", [128, 1])
    wgt = din("wgt", [DM, E])
    w1 = din("w1", [E * DM, DM])
    w3 = din("w3", [E * DM, DM])
    w2 = din("w2", [E * DM, DM])
    sw1 = din("sw1", [DM, DM])
    sw3 = din("sw3", [DM, DM])
    sw2 = din("sw2", [DM, DM])
    din("lstT", [128, 128])
    din("clocb", [128, E])
    din("iota8", [128, 8])
    din("maskc", [8, 1])
    dout("out", [NTOK, DM])

    wz_i = nc.dram_tensor("wz_i", [DM, DI], BF16)
    wo_i = nc.dram_tensor("wo_i", [DI, DM], BF16)
    w1_i = nc.dram_tensor("w1_i", [E * 128, 8 * DM], BF16)
    w3_i = nc.dram_tensor("w3_i", [E * 128, 8 * DM], BF16)
    w2_i = nc.dram_tensor("w2_i", [E * 128, 8 * DM], BF16)
    ebuf = nc.dram_tensor("ebuf", [EB_ROWS, DM], BF16)
    rec = nc.dram_tensor("rec", [EB_ROWS, 2], F32)
    xnw = nc.dram_tensor("xnw", [NTOK, DM], F32)
    cntin = nc.dram_tensor("cntin", [1, E], F32)
    cnt8o = nc.dram_tensor("cnt8o", [8, E], F32)

    es = ExitStack()
    S = es.enter_context
    with es:
        identbs = S(nc.sbuf_tensor("identbs", [128, 128], BF16))
        onesrs = S(nc.sbuf_tensor("onesrs", [1, 128], F32))
        onecs = S(nc.sbuf_tensor("onecs", [128, 1], F32))
        epsc = S(nc.sbuf_tensor("epsc", [128, 1], F32))
        cols = S(nc.sbuf_tensor("cols", [128, 8], F32))
        hwork = S(nc.sbuf_tensor("hwork", [128, DM], F32))
        arena = S(nc.sbuf_tensor("arena", [128, 3072], F32))
        zbf = S(nc.sbuf_tensor("zbf", [128, 2048], BF16))
        zerot = S(nc.sbuf_tensor("zerot", [128, 576], F32))

        P = Prog(nc, es)
        ld, gq, sq, ts, vs, ss, ost = "ld", "gq", "sq", "ts", "vs", "ss", "ld"

        esA = ExitStack()
        A = esA.enter_context
        wxbc = A(nc.sbuf_tensor("wxbc", [128, 8, WXW], BF16))
        cdg = A(nc.sbuf_tensor("cdg", [128, 4 * NCT, 128], BF16))
        cbs = A(nc.sbuf_tensor("cbs", [128, NCT], F32))
        dtbs = A(nc.sbuf_tensor("dtbs", [32, 1], F32))
        anegs = A(nc.sbuf_tensor("anegs", [32, 1], F32))
        dbcs = A(nc.sbuf_tensor("dbcs", [128, DI], BF16))
        rmsbs = A(nc.sbuf_tensor("rmsbs", [128, DI], BF16))
        lngs = A(nc.sbuf_tensor("lngs", [128, DM], BF16))
        lnbs = A(nc.sbuf_tensor("lnbs", [128, DM], BF16))
        sels = A(nc.sbuf_tensor("sels", [128, 1], F32))
        diagms = A(nc.sbuf_tensor("diagms", [128, 8 * 128], BF16))
        bigms = A(nc.sbuf_tensor("bigms", [128, 128], F32))
        ones128s = A(nc.sbuf_tensor("ones128s", [128, 128], F32))
        identfs = A(nc.sbuf_tensor("identfs", [128, 128], F32))
        oh127s = A(nc.sbuf_tensor("oh127s", [128, 1], F32))
        stgb = A(nc.sbuf_tensor("stgb", [128, 2048], BF16))
        wzb = [A(nc.sbuf_tensor(f"wzb{i}", [128, 8, 512], BF16)) for i in range(2)]
        wob = [A(nc.sbuf_tensor(f"wob{i}", [128, 512], BF16)) for i in range(2)]
        xtile = A(nc.sbuf_tensor("xtile", [128, LT // 128, DM], F32))
        lnxtok = A(nc.sbuf_tensor("lnxtok", [128, LT // 128, DM], BF16))
        lnxT = A(nc.sbuf_tensor("lnxT", [128, 8, LT], BF16))
        halT = A(nc.sbuf_tensor("halT", [128, 8, 3], BF16))
        sdz = A(nc.sbuf_tensor("sdz", [128, LT // 128, DI], BF16))
        xbcT = A(nc.sbuf_tensor("xbcT", [128, NCT, LT + 3], BF16))
        dtpsT = A(nc.sbuf_tensor("dtpsT", [32, LT], F32))
        xchTs2 = A(nc.sbuf_tensor("xchTs2", [128, 2, LT], BF16))
        btct = A(nc.sbuf_tensor("btct", [128, 2, LT], BF16))
        xtok = A(nc.sbuf_tensor("xtok", [128, LT // 128, DI], BF16))
        btok = A(nc.sbuf_tensor("btok", [128, LT // 128, DST], BF16))
        xprime = A(nc.sbuf_tensor("xprime", [128, DI], BF16))
        dtT = A(nc.sbuf_tensor("dtT", [32, LT], F32))
        acT = A(nc.sbuf_tensor("acT", [32, LT], F32))
        lndT = A(nc.sbuf_tensor("lndT", [32, LT], F32))
        biasT = A(nc.sbuf_tensor("biasT", [32, LT], F32))
        wdT = A(nc.sbuf_tensor("wdT", [32, LT], F32))
        stk = A(nc.sbuf_tensor("stk", [96, 128], F32))
        tokc = A(nc.sbuf_tensor("tokc", [128, 96], F32))
        utok = A(nc.sbuf_tensor("utok", [128, 32], F32))
        drow = A(nc.sbuf_tensor("drow", [1, 32], F32))
        mpall = A(nc.sbuf_tensor("mpall", [128, 8 * 128], BF16))
        gev = A(nc.sbuf_tensor("gev", [128, 128], BF16))
        prev = A(nc.sbuf_tensor("prev", [128, DI], F32))
        prevb = A(nc.sbuf_tensor("prevb", [128, DI], BF16))
        ysb = A(nc.sbuf_tensor("ysb", [128, DI], F32))
        expq = A(nc.sbuf_tensor("expq", [128, 1024], BF16))
        ynbf = A(nc.sbuf_tensor("ynbf", [128, DI], BF16))
        ynT = A(nc.sbuf_tensor("ynT", [128, 16, LT], BF16))

        pA = A(nc.psum_tensor("pA", [128, 512], F32))
        pB = A(nc.psum_tensor("pB", [128, 512], F32))
        pT = A(nc.psum_tensor("pT", [128, 512], BF16))
        pG = A(nc.psum_tensor("pG", [128, 128], F32))
        pR = A(nc.psum_tensor("pR", [128, 1024], F32))
        pY = A(nc.psum_tensor("pY", [128, 512], F32))
        pM = A(nc.psum_tensor("pM", [128, 512], F32))

        P.op("vector", lambda e: e.memset(zbf[:], 0.0), vs)
        P.op("vector", lambda e: e.memset(zerot[:], 0.0), vs)
        P.op("vector", lambda e: e.memset(epsc[:], EPS), vs)
        P.op("vector", lambda e: e.memset(cols[:], 0.0), vs)
        P.op("vector", lambda e: e.memset(prev[:], 0.0), vs)
        P.wait("scalar", vs, P.val(vs))
        for r in range(0, EB_ROWS - 255, 256):
            P.op("scalar", lambda e, r=r: e.dma_start(
                out=ebuf[r:r + 256, :].rearrange("(a p) n -> p a n", p=128),
                in_=zbf[:, :].rearrange("p (a n) -> p a n", a=2)), sq, 16)
        P.op("scalar", lambda e: e.dma_start(
            out=ebuf[EB_ROWS - 128:EB_ROWS, :], in_=zbf[:, 0:1024]), sq, 16)
        P.op("scalar", lambda e: e.dma_start(
            out=rec[0:EB_ROWS, :].rearrange("(a p) n -> p a n", p=128),
            in_=zerot[:, 0:(EB_ROWS // 128) * 2].rearrange("p (a n) -> p a n", n=2)), sq, 16)
        zval = P.val(sq)

        for name, dst in [("dbc", dbcs), ("rmsbc", rmsbs), ("lngbc", lngs), ("lnbbc", lnbs)]:
            P.op("gpsimd", lambda e, a=ap[name], d=dst: e.dma_start(out=d[:], in_=a[:, :]), gq, 16)
        for name, dst in [("convb", cbs), ("dtb", dtbs),
                          ("selc", sels), ("diagm", diagms), ("bigm", bigms),
                          ("ones128", ones128s), ("identf", identfs),
                          ("identb", identbs), ("onesr", onesrs), ("onescol", onecs),
                          ("oh127", oh127s), ("alog", anegs)]:
            P.op("sync", lambda e, a=ap[name], d=dst: e.dma_start(out=d[:], in_=a[:, :]), ld, 16)
        P.op("gpsimd", lambda e: e.dma_start(
            out=wxbc[:], in_=win[:, DI:DI + WXW].rearrange("(k p) n -> p k n", p=128)), gq, 16)
        P.op("gpsimd", lambda e: e.dma_start(
            out=cdg[:], in_=convd[:, :].rearrange("(t p) n -> p t n", p=128)), gq, 16)
        for k in range(8):
            P.op("gpsimd", lambda e, k=k: e.dma_start(
                out=stgb[:], in_=win[k * 128:(k + 1) * 128, 0:DI]), gq, 16)
            P.wait("sync", gq, P.val(gq))
            P.op("sync", lambda e, k=k: e.dma_start(
                out=wz_i[k * 128:(k + 1) * 128, :], in_=stgb[:]), ld, 16)
            P.wait("gpsimd", ld, P.val(ld))
        for k in range(16):
            P.op("gpsimd", lambda e, k=k: e.dma_start(
                out=stgb[:, 0:DM], in_=wout[k * 128:(k + 1) * 128, 0:DM]), gq, 16)
            P.wait("sync", gq, P.val(gq))
            P.op("sync", lambda e, k=k: e.dma_start(
                out=wo_i[k * 128:(k + 1) * 128, :], in_=stgb[:, 0:DM]), ld, 16)
            P.wait("gpsimd", ld, P.val(ld))
        P.wait("scalar", ld, P.val(ld))
        P.op("scalar", lambda e: e.activation(out=anegs[:], in_=anegs[:], func=AF.Exp), ss)
        P.wait("vector", ss, P.val(ss))
        P.wait("vector", gq, P.val(gq))
        P.op("vector", lambda e: e.tensor_scalar_mul(out=anegs[:], in0=anegs[:], scalar1=-1.0), vs)
        P.wait("tensor", ld, P.val(ld))
        P.wait("tensor", gq, P.val(gq))
        P.wait("tensor", vs, P.val(vs))
        P.wait("scalar", vs, P.val(vs))
        stage_q = []
        for eix in range(E):
            for src, dst in ((w1, w1_i), (w3, w3_i), (w2, w2_i)):
                stage_q.append(lambda e, src=src, dst=dst, eix=eix: e.dma_start(
                    out=dst[eix * 128:(eix + 1) * 128, :].rearrange("p (k n) -> p k n", k=8),
                    in_=src[eix * DM:(eix + 1) * DM, :].rearrange("(k p) n -> p k n", p=128)))

        def drain_stage(nmax):
            for _ in range(nmax):
                if stage_q:
                    P.op("gpsimd", stage_q.pop(0), gq, 16)

        def ln_rows(src_ap, dst_ap, r):
            P.op("vector", lambda e: e.tensor_reduce(out=cols[0:r, 0:1], in_=src_ap, axis=AX, op=ALU.add), vs)
            P.op("vector", lambda e: e.tensor_scalar_mul(out=cols[0:r, 0:1], in0=cols[0:r, 0:1], scalar1=1.0 / DM), vs)
            P.op("vector", lambda e: e.tensor_scalar(
                out=arena[0:r, 2048:3072], in0=src_ap, scalar1=cols[0:r, 0:1], scalar2=None, op0=ALU.subtract), vs)
            P.wait("scalar", vs, P.val(vs))
            P.op("scalar", lambda e: e.activation(
                out=arena[0:r, 1024:2048], in_=arena[0:r, 2048:3072], func=AF.Square, accum_out=cols[0:r, 1:2]), ss)
            P.op("scalar", lambda e: e.activation(
                out=cols[0:r, 2:3], in_=cols[0:r, 1:2], func=AF.Ln, scale=1.0 / DM, bias=epsc[0:r, 0:1]), ss)
            P.op("scalar", lambda e: e.activation(
                out=cols[0:r, 2:3], in_=cols[0:r, 2:3], func=AF.Exp, scale=-0.5), ss)
            P.wait("vector", ss, P.val(ss))
            P.op("vector", lambda e: e.tensor_scalar(
                out=dst_ap, in0=arena[0:r, 2048:3072], scalar1=cols[0:r, 2:3], scalar2=None, op0=ALU.mult), vs)

        def transpose_128(src_ap, dst_ap, cols_n=128):
            P.wait("tensor", vs, P.val(vs))
            P.op("tensor", lambda e: e.transpose(out=pT[:, 0:cols_n], in_=src_ap, identity=identbs[0:cols_n, 0:cols_n]), ts)
            P.wait("vector", ts, P.val(ts))
            P.op("vector", lambda e: e.tensor_copy(out=dst_ap, in_=pT[:, 0:cols_n]), vs)

        def transpose_batch(srcs, dst3):
            n = len(srcs)
            P.wait("tensor", vs, P.val(vs))
            for i, s in enumerate(srcs):
                P.op("tensor", lambda e, s=s, i=i: e.transpose(
                    out=pT[:, i * 128:(i + 1) * 128], in_=s, identity=identbs[:]),
                    ts if i == n - 1 else None)
            P.wait("vector", ts, P.val(ts))
            P.op("vector", lambda e, dst3=dst3, n=n: e.tensor_copy(
                out=dst3, in_=pT[:, 0:n * 128].rearrange("p (k x) -> p k x", k=n)), vs)

        def halo_block(src):
            P.wait("sync", vs, P.val(vs))
            P.op("sync", lambda e: e.dma_start(out=xtile[0:3, 0, :], in_=src[0:3, :]), ld, 16)
            P.wait("vector", ld, P.val(ld))
            ln_rows(xtile[0:3, 0, :], lnxtok[0:3, 0, :], 3)
            P.wait("tensor", vs, P.val(vs))
            for k in range(8):
                transpose_128(lnxtok[0:3, 0, k * 128:(k + 1) * 128], halT[:, k, :], 3)
            P.wait("tensor", vs, P.val(vs))
            for ct in range(NCT):
                P.guard("pA")
                for k in range(8):
                    P.op("tensor", lambda e, ct=ct, k=k: e.matmul(
                        out=pA[:, 0:3], lhsT=wxbc[:, k, ct * 128:(ct + 1) * 128],
                        rhs=halT[:, k, :], start=(k == 0), stop=(k == 7)),
                        ts if k == 7 else None)
                P.wait("vector", ts, P.val(ts))
                P.op("vector", lambda e, ct=ct: e.tensor_copy(out=xbcT[:, ct, 0:3], in_=pA[:, 0:3]), vs)
                P.evac("pA", vs)

        def dt_pipeline():
            P.op("vector", lambda e: e.tensor_scalar_add(out=dtT[:], in0=dtpsT[:], scalar1=dtbs[:, 0:1]), vs)
            P.wait("scalar", vs, P.val(vs))
            P.op("scalar", lambda e: e.activation(out=lndT[:], in_=dtT[:], func=AF.Abs), ss)
            P.op("scalar", lambda e: e.activation(out=lndT[:], in_=lndT[:], func=AF.Exp, scale=-1.0), ss)
            P.op("scalar", lambda e: e.activation(out=lndT[:], in_=lndT[:], func=AF.Ln, bias=onecs[0:32, 0:1]), ss)
            P.wait("vector", ss, P.val(ss))
            P.op("vector", lambda e: e.tensor_scalar_max(out=dtT[:], in0=dtT[:], scalar1=0.0), vs)
            P.op("vector", lambda e: e.tensor_add(out=dtT[:], in0=dtT[:], in1=lndT[:]), vs)
            P.op("vector", lambda e: e.tensor_scalar(
                out=acT[:], in0=dtT[:], scalar1=anegs[:, 0:1], scalar2=None, op0=ALU.mult), vs)
            for c in range(LT // CH):
                P.op("vector", lambda e, c=c: e.tensor_tensor_scan(
                    out=acT[:, c * CH:(c + 1) * CH], data0=acT[:, c * CH:(c + 1) * CH],
                    data1=acT[:, c * CH:(c + 1) * CH], initial=0.0, op0=ALU.add, op1=ALU.bypass), vs)
            P.wait("scalar", vs, P.val(vs))
            P.op("scalar", lambda e: e.activation(out=lndT[:], in_=dtT[:], func=AF.Ln), ss)
            P.wait("vector", ss, P.val(ss))
            P.op("vector", lambda e: e.tensor_sub(out=biasT[:], in0=lndT[:], in1=acT[:]), vs)
            for c in range(LT // CH):
                P.op("vector", lambda e, c=c: e.tensor_scalar_add(
                    out=wdT[:, c * CH:(c + 1) * CH], in0=biasT[:, c * CH:(c + 1) * CH],
                    scalar1=acT[:, c * CH + CH - 1:c * CH + CH]), vs)
            P.wait("scalar", vs, P.val(vs))
            P.op("scalar", lambda e: e.activation(out=wdT[:], in_=wdT[:], func=AF.Exp), ss)
            P.wait("vector", ss, P.val(ss))

        def chunk_ssd(c, co, full):
            P.op("vector", lambda e, co=co: e.tensor_copy(out=stk[0:32, :], in_=biasT[:, co:co + CH]), vs)
            P.op("vector", lambda e, co=co: e.tensor_copy(out=stk[32:64, :], in_=acT[:, co:co + CH]), vs)
            P.op("vector", lambda e, co=co: e.tensor_copy(out=stk[64:96, :], in_=wdT[:, co:co + CH]), vs)
            P.wait("tensor", vs, P.val(vs))
            P.guard("pM")
            P.op("tensor", lambda e: e.transpose(out=pM[:, 0:96], in_=stk[:, :], identity=identfs[0:96, 0:96]), ts)
            P.wait("vector", ts, P.val(ts))
            P.op("vector", lambda e: e.tensor_copy(out=tokc[:], in_=pM[:, 0:96]), vs)
            P.evac("pM", vs)
            P.wait("scalar", vs, P.val(vs))
            P.op("scalar", lambda e: e.activation(out=utok[:], in_=tokc[:, 32:64], func=AF.Exp), ss)
            P.wait("tensor", vs, P.val(vs))
            P.guard("pG")
            P.op("tensor", lambda e: e.matmul(out=pG[0:1, 0:32], lhsT=oh127s[:, 0:1],
                                              rhs=tokc[:, 32:64], start=True, stop=True), ts)
            P.wait("scalar", ts, P.val(ts))
            P.op("scalar", lambda e: e.activation(out=drow[:], in_=pG[0:1, 0:32], func=AF.Exp), ss)
            P.evac("pG", ss)
            P.wait("vector", ss, P.val(ss))
            P.op("vector", lambda e: e.tensor_copy(
                out=arena[0:1, 1024:3072].rearrange("a (h p) -> a h p", h=32),
                in_=drow[:, :].rearrange("a h -> a h ()").broadcast_to((1, 32, 64))), vs)
            P.op("vector", lambda e, c=c: e.tensor_tensor(
                out=xprime[:].rearrange("p (h x) -> p h x", h=32),
                in0=xtok[:, c, :].rearrange("p (h x) -> p h x", h=32),
                in1=tokc[:, 64:96].rearrange("p h -> p h ()").broadcast_to((128, 32, 64)),
                op=ALU.mult), vs)

            if full:
                P.wait("tensor", vs, P.val(vs))
                P.guard("pG")
                P.op("tensor", lambda e, co=co: e.matmul(
                    out=pG[:, :], lhsT=btct[:, 0, co:co + CH], rhs=btct[:, 1, co:co + CH],
                    start=True, stop=True), ts)
                P.wait("vector", ts, P.val(ts))
                P.op("vector", lambda e: e.tensor_copy(out=gev[:], in_=pG[:, :]), vs)
                P.evac("pG", vs)
                P.op("vector", lambda e, c=c: e.tensor_tensor(
                    out=ysb[:], in0=xtok[:, c, :], in1=dbcs[:], op=ALU.mult), vs)
                for q in range(4):
                    P.op("vector", lambda e, q=q: e.tensor_tensor(
                        out=arena[:, 0:1024].rearrange("p (h l) -> p h l", h=8),
                        in0=tokc[:, 32 + q * 8:32 + (q + 1) * 8].rearrange("p h -> p h ()").broadcast_to((128, 8, 128)),
                        in1=diagms[:, :].rearrange("p (h l) -> p h l", h=8),
                        op=ALU.mult), vs)
                    P.wait("tensor", vs, P.val(vs))
                    P.guard("pR")
                    for n in range(2):
                        P.op("tensor", lambda e, n=n: e.matmul(
                            out=pR[:, n * 512:(n + 1) * 512], lhsT=ones128s[:],
                            rhs=arena[:, n * 512:n * 512 + 512], start=True, stop=False), None)
                        P.op("tensor", lambda e, q=q, n=n: e.matmul(
                            out=pR[:, n * 512:(n + 1) * 512], lhsT=identfs[:],
                            rhs=tokc[:, q * 8 + n * 4:q * 8 + (n + 1) * 4].rearrange(
                                "p h -> p h ()").broadcast_to((128, 4, 128)),
                            start=False, stop=False), None)
                        P.op("tensor", lambda e, n=n: e.matmul(
                            out=pR[:, n * 512:(n + 1) * 512], lhsT=identfs[:],
                            rhs=bigms[:, :].rearrange("p l -> p () l").broadcast_to((128, 4, 128)),
                            start=False, stop=True), ts)
                    P.wait("scalar", ts, P.val(ts))
                    P.op("scalar", lambda e: e.activation(out=expq[:], in_=pR[:, 0:1024], func=AF.Exp), ss)
                    P.evac("pR", ss)
                    P.wait("vector", ss, P.val(ss))
                    P.op("vector", lambda e, q=q: e.tensor_tensor(
                        out=mpall[:, :].rearrange("p (h l) -> p h l", h=8),
                        in0=expq[:].rearrange("p (h l) -> p h l", h=8),
                        in1=gev[:, :].rearrange("p l -> p () l").broadcast_to((128, 8, 128)),
                        op=ALU.mult), vs)
                    P.wait("tensor", vs, P.val(vs))
                    for hh in range(8):
                        h = q * 8 + hh
                        P.op("tensor", lambda e, h=h, hh=hh, c=c: e.matmul(
                            out=pY[:, hh * 64:(hh + 1) * 64], lhsT=mpall[:, hh * 128:(hh + 1) * 128],
                            rhs=xtok[:, c, h * 64:(h + 1) * 64], start=True, stop=True),
                            ts if hh == 7 else None)
                    P.wait("vector", ts, P.val(ts))
                    P.op("vector", lambda e, q=q: e.tensor_tensor(
                        out=ysb[:, q * 512:(q + 1) * 512], in0=ysb[:, q * 512:(q + 1) * 512],
                        in1=pY[:, :], op=ALU.add), vs)
                P.op("vector", lambda e: e.tensor_copy(out=prevb[:], in_=prev[:]), vs)
                P.wait("tensor", vs, P.val(vs))
                for n in range(4):
                    P.guard("pM")
                    P.op("tensor", lambda e, n=n, co=co: e.matmul(
                        out=pM[:, 0:512], lhsT=btct[:, 1, co:co + CH],
                        rhs=prevb[:, n * 512:(n + 1) * 512], start=True, stop=True), ts)
                    P.wait("vector", ts, P.val(ts))
                    P.op("vector", lambda e, n=n: e.tensor_tensor(
                        out=arena[:, 1024:1536].rearrange("p (h x) -> p h x", h=8),
                        in0=pM[:, 0:512].rearrange("p (h x) -> p h x", h=8),
                        in1=utok[:, n * 8:(n + 1) * 8].rearrange("p h -> p h ()").broadcast_to((128, 8, 64)),
                        op=ALU.mult), vs)
                    P.evac("pM", vs)
                    P.op("vector", lambda e, n=n: e.tensor_tensor(
                        out=ysb[:, n * 512:(n + 1) * 512], in0=ysb[:, n * 512:(n + 1) * 512],
                        in1=arena[:, 1024:1536], op=ALU.add), vs)

            P.wait("tensor", vs, P.val(vs))
            for n in range(4):
                P.guard("pA")
                P.guard("pB")
                P.op("tensor", lambda e, n=n: e.matmul(
                    out=pA[:, 0:512], lhsT=onesrs[:, :], rhs=arena[0:1, 1024 + n * 512:1024 + (n + 1) * 512],
                    start=True, stop=True), ts)
                P.op("tensor", lambda e, n=n, c=c: e.matmul(
                    out=pB[:, 0:512], lhsT=btok[:, c, :], rhs=xprime[:, n * 512:(n + 1) * 512],
                    start=True, stop=True), ts)
                P.wait("vector", ts, P.val(ts))
                P.op("vector", lambda e, n=n: e.tensor_tensor(
                    out=arena[:, 1536:2048], in0=prev[:, n * 512:(n + 1) * 512],
                    in1=pA[:, 0:512], op=ALU.mult), vs)
                P.evac("pA", vs)
                P.op("vector", lambda e, n=n: e.tensor_tensor(
                    out=prev[:, n * 512:(n + 1) * 512], in0=arena[:, 1536:2048],
                    in1=pB[:, 0:512], op=ALU.add), vs)
                P.evac("pB", vs)

            if full:
                P.op("vector", lambda e, c=c: e.tensor_tensor(
                    out=arena[:, 1024:3072], in0=ysb[:], in1=sdz[:, c, :], op=ALU.mult), vs)
                P.wait("scalar", vs, P.val(vs))
                P.op("scalar", lambda e: e.activation(
                    out=ysb[:], in_=arena[:, 1024:3072], func=AF.Square, accum_out=cols[:, 3:4]), ss)
                P.op("scalar", lambda e: e.activation(
                    out=cols[:, 4:5], in_=cols[:, 3:4], func=AF.Ln, scale=1.0 / DI, bias=epsc[:, 0:1]), ss)
                P.op("scalar", lambda e: e.activation(
                    out=cols[:, 4:5], in_=cols[:, 4:5], func=AF.Exp, scale=-0.5), ss)
                P.wait("vector", ss, P.val(ss))
                P.op("vector", lambda e: e.tensor_scalar(
                    out=arena[:, 1024:3072], in0=arena[:, 1024:3072], scalar1=cols[:, 4:5],
                    scalar2=None, op0=ALU.mult), vs)
                P.op("vector", lambda e: e.tensor_tensor(
                    out=ynbf[:], in0=arena[:, 1024:3072], in1=rmsbs[:], op=ALU.mult), vs)
                P.wait("tensor", vs, P.val(vs))
                for tq0 in range(0, 16, 4):
                    transpose_batch(
                        [ynbf[:, (tq0 + i) * 128:(tq0 + i + 1) * 128] for i in range(4)],
                        ynT[:, tq0:tq0 + 4, co:co + 128])

        def z_inproj(it):
            for n in range(4):
                buf = wzb[n % 2]
                if n >= 2:
                    P.wait("sync", ts, P.val(ts))
                P.op("sync", lambda e, n=n, buf=buf: e.dma_start(
                    out=buf[:], in_=wz_i[:, n * 512:(n + 1) * 512].rearrange("(k p) n -> p k n", p=128)), ld, 16)
                P.wait("tensor", ld, P.val(ld))
                for j in range(LT // 128):
                    bk, bn = (pB, "pB") if (2 * n + j) % 2 == 0 else (pA, "pA")
                    P.guard(bn)
                    for k in range(8):
                        P.op("tensor", lambda e, j=j, k=k, buf=buf, bk=bk: e.matmul(
                            out=bk[:, 0:512], lhsT=lnxT[:, k, j * 128:j * 128 + 128],
                            rhs=buf[:, k, :], start=(k == 0), stop=(k == 7)),
                            ts if k == 7 else None)
                    P.wait("scalar", ts, P.val(ts))
                    P.op("scalar", lambda e, j=j, n=n, bk=bk: e.activation(
                        out=sdz[:, j, n * 512:(n + 1) * 512], in_=bk[:, 0:512], func=AF.Silu), ss)
                    P.evac(bn, ss)
            P.wait("tensor", ss, P.val(ss))

        def outproj_tile(it):
            o = it * LT
            mmv = []
            for bn in ("pA", "pB", "pY", "pM"):
                P.guard(bn)
            for n in range(2):
                for k in range(16):
                    buf = wob[k % 2]
                    if len(mmv) >= 2:
                        P.wait("sync", ts, mmv[-2])
                    P.op("sync", lambda e, k=k, n=n, buf=buf: e.dma_start(
                        out=buf[:], in_=wo_i[k * 128:(k + 1) * 128, n * 512:(n + 1) * 512]), ld, 16)
                    P.wait("tensor", ld, P.val(ld))
                    P.op("tensor", lambda e, k=k, n=n, buf=buf: e.matmul(
                        out=(pA if n == 0 else pB)[:, 0:512],
                        lhsT=ynT[:, k, 0:128],
                        rhs=buf[:], start=(k == 0), stop=(k == 15)), None)
                    v = P.op("tensor", lambda e, k=k, n=n, buf=buf: e.matmul(
                        out=(pY if n == 0 else pM)[:, 0:512],
                        lhsT=ynT[:, k, 128:256],
                        rhs=buf[:], start=(k == 0), stop=(k == 15)), ts)
                    mmv.append(v)
            for j in range(LT // 128):
                bkA, bnA, bkB, bnB = (pA, "pA", pB, "pB") if j == 0 else (pY, "pY", pM, "pM")
                P.wait("vector", ts, P.val(ts))
                P.wait("vector", ost, P.val(ost))
                P.op("vector", lambda e, j=j, bkA=bkA: e.tensor_tensor(
                    out=hwork[:, 0:512], in0=bkA[:, 0:512], in1=lnxtok[:, j, 0:512], op=ALU.add), vs)
                P.op("vector", lambda e, j=j, bkB=bkB: e.tensor_tensor(
                    out=hwork[:, 512:1024], in0=bkB[:, 0:512], in1=lnxtok[:, j, 512:1024], op=ALU.add), vs)
                P.evac(bnA, vs)
                P.evac(bnB, vs)
                P.op("vector", lambda e: e.tensor_reduce(out=cols[:, 0:1], in_=hwork[:], axis=AX, op=ALU.add), vs)
                P.op("vector", lambda e: e.tensor_scalar_mul(out=cols[:, 0:1], in0=cols[:, 0:1], scalar1=1.0 / DM), vs)
                P.op("vector", lambda e: e.tensor_scalar(
                    out=arena[:, 0:DM], in0=hwork[:], scalar1=cols[:, 0:1], scalar2=None, op0=ALU.subtract), vs)
                P.wait("scalar", vs, P.val(vs))
                P.op("scalar", lambda e: e.activation(
                    out=arena[:, 1024:2048], in_=arena[:, 0:DM], func=AF.Square, accum_out=cols[:, 1:2]), ss)
                P.op("scalar", lambda e: e.activation(
                    out=cols[:, 2:3], in_=cols[:, 1:2], func=AF.Ln, scale=1.0 / DM, bias=epsc[:, 0:1]), ss)
                P.op("scalar", lambda e: e.activation(
                    out=cols[:, 2:3], in_=cols[:, 2:3], func=AF.Exp, scale=-0.5), ss)
                P.wait("vector", ss, P.val(ss))
                P.op("vector", lambda e: e.tensor_scalar(
                    out=arena[:, 0:DM], in0=arena[:, 0:DM], scalar1=cols[:, 2:3], scalar2=None, op0=ALU.mult), vs)
                P.op("vector", lambda e: e.tensor_tensor(out=arena[:, 0:DM], in0=arena[:, 0:DM], in1=lngs[:], op=ALU.mult), vs)
                P.op("vector", lambda e: e.tensor_tensor(out=arena[:, 0:DM], in0=arena[:, 0:DM], in1=lnbs[:], op=ALU.add), vs)
                P.op("vector", lambda e, j=j: e.tensor_tensor(
                    out=hwork[:], in0=arena[:, 0:DM], in1=xtile[:, j, :], op=ALU.add), vs)
                P.wait("sync", vs, P.val(vs))
                P.op("sync", lambda e, o=o, j=j: e.dma_start(
                    out=xnw[o + j * 128:o + (j + 1) * 128, :], in_=hwork[:]), ost, 16)

        def attn_pass(src, full):
            halo_block(src)
            for it in range(NLT):
                drain_stage(9)
                o = it * LT
                P.wait("sync", vs, P.val(vs))
                P.op("sync", lambda e, o=o: e.dma_start(
                    out=xtile[:], in_=src[3 + o:3 + o + LT, :].rearrange("(s p) n -> p s n", p=128)), ld, 16)
                P.wait("vector", ld, P.val(ld))
                for j in range(LT // 128):
                    ln_rows(xtile[:, j, :], lnxtok[:, j, :], 128)
                P.wait("tensor", vs, P.val(vs))
                for j in range(LT // 128):
                    for k0 in (0, 4):
                        transpose_batch(
                            [lnxtok[:, j, (k0 + i) * 128:(k0 + i + 1) * 128] for i in range(4)],
                            lnxT[:, k0:k0 + 4, j * 128:j * 128 + 128])
                if it > 0:
                    P.op("vector", lambda e: e.tensor_copy(
                        out=xbcT[:, 0:NCT, 0:3], in_=xbcT[:, 0:NCT, LT:LT + 3]), vs)
                P.wait("tensor", vs, P.val(vs))
                for cp in range(0, NCT, 2):
                    bk, bn = (pA, "pA") if (cp // 2) % 2 == 0 else (pB, "pB")
                    P.guard(bn)
                    for ci in range(2):
                        ct = cp + ci
                        for k in range(8):
                            P.op("tensor", lambda e, ct=ct, k=k, ci=ci, bk=bk: e.matmul(
                                out=bk[:, ci * LT:ci * LT + LT], lhsT=wxbc[:, k, ct * 128:ct * 128 + 128],
                                rhs=lnxT[:, k, :], start=(k == 0), stop=(k == 7)),
                                ts if (k == 7 and ci == 1) else None)
                    P.wait("vector", ts, P.val(ts))
                    P.op("vector", lambda e, cp=cp, bk=bk: e.tensor_copy(
                        out=xbcT[:, cp:cp + 2, 3:3 + LT],
                        in_=bk[:, 0:2 * LT].rearrange("p (c x) -> p c x", c=2)), vs)
                    P.evac(bn, vs)
                bk, bn = (pA, "pA") if (NCT // 2) % 2 == 0 else (pB, "pB")
                P.guard(bn)
                for k in range(8):
                    P.op("tensor", lambda e, k=k, bk=bk: e.matmul(
                        out=bk[0:32, 0:LT], lhsT=wxbc[:, k, NCT * 128:NCT * 128 + 32],
                        rhs=lnxT[:, k, :], start=(k == 0), stop=(k == 7)),
                        ts if k == 7 else None)
                P.wait("vector", ts, P.val(ts))
                P.op("vector", lambda e, bk=bk: e.tensor_copy(out=dtpsT[:], in_=bk[0:32, 0:LT]), vs)
                P.evac(bn, vs)
                if full:
                    z_inproj(it)
                P.wait("tensor", vs, P.val(vs))
                for cp in range(0, NCT, 2):
                    bk, bn = (pB, "pB") if (cp // 2) % 2 == 0 else (pA, "pA")
                    P.guard(bn)
                    for ci in range(2):
                        ct = cp + ci
                        for tap in range(4):
                            P.op("tensor", lambda e, ct=ct, tap=tap, ci=ci, bk=bk: e.matmul(
                                out=bk[:, ci * LT:ci * LT + LT], lhsT=cdg[:, tap * NCT + ct, :],
                                rhs=xbcT[:, ct, tap:tap + LT], start=(tap == 0), stop=(tap == 3)),
                                ts if (tap == 3 and ci == 1) else None)
                    P.wait("scalar", ts, P.val(ts))
                    if cp < 16:
                        for ci in range(2):
                            P.op("scalar", lambda e, ci=ci, cp=cp, bk=bk: e.activation(
                                out=xchTs2[:, ci, :], in_=bk[:, ci * LT:ci * LT + LT],
                                func=AF.Silu, bias=cbs[:, cp + ci:cp + ci + 1]), ss)
                        P.evac(bn, ss)
                        P.wait("tensor", ss, P.val(ss))
                        P.wait("tensor", vs, P.val(vs))
                        srcs = [xchTs2[:, 0, 0:128], xchTs2[:, 1, 0:128],
                                xchTs2[:, 0, 128:256], xchTs2[:, 1, 128:256]]
                        for i, sap in enumerate(srcs):
                            P.op("tensor", lambda e, sap=sap, i=i: e.transpose(
                                out=pT[:, i * 128:(i + 1) * 128], in_=sap, identity=identbs[:]),
                                ts if i == 3 else None)
                        P.wait("vector", ts, P.val(ts))
                        P.op("vector", lambda e, cp=cp: e.tensor_copy(
                            out=xtok[:, 0:2, cp * 128:(cp + 2) * 128].rearrange(
                                "p j (c x) -> p j c x", c=2),
                            in_=pT[:, 0:512].rearrange("p (j c x) -> p j c x", j=2, c=2)), vs)
                    else:
                        for ci in range(2):
                            P.op("scalar", lambda e, ci=ci, cp=cp, bk=bk: e.activation(
                                out=btct[:, ci, 0:LT], in_=bk[:, ci * LT:ci * LT + LT],
                                func=AF.Silu, bias=cbs[:, cp + ci:cp + ci + 1]), ss)
                        P.evac(bn, ss)
                P.wait("tensor", ss, P.val(ss))
                transpose_batch(
                    [btct[:, 0, j * 128:j * 128 + 128] for j in range(LT // 128)],
                    btok[:, 0:LT // 128, :])
                dt_pipeline()
                for c in range(LT // CH):
                    chunk_ssd(c, c * CH, full)
                if full:
                    outproj_tile(it)

        attn_pass(xp, full=False)
        P.op("vector", lambda e: e.tensor_scalar(
            out=prev[:], in0=prev[:], scalar1=sels[:, 0:1], scalar2=None, op0=ALU.mult), vs)
        attn_pass(xo, full=True)

        for eng in ENGS:
            for s in (ld, gq, sq, ts, vs, ss):
                P.wait(eng, s)
        esA.close()

        iota8s = S(nc.sbuf_tensor("iota8s", [128, 8], F32))
        maskcs = S(nc.sbuf_tensor("maskcs", [8, 1], F32))
        cnt8s = S(nc.sbuf_tensor("cnt8s", [8, E], F32))
        bases = S(nc.sbuf_tensor("bases", [1, E], F32))
        basen = S(nc.sbuf_tensor("basen", [128, E], F32))
        runA = S(nc.sbuf_tensor("runA", [1, E], F32))
        ysc = [S(nc.sbuf_tensor(f"ysc{i}", [128, DM], F32)) for i in range(2)]
        ind8 = S(nc.sbuf_tensor("ind8", [128, 8], F32))
        recs = [S(nc.sbuf_tensor(f"recs{i}", [128, 2], F32)) for i in range(2)]
        tu32 = [S(nc.sbuf_tensor(f"tu32{i}", [128, 1], U32)) for i in range(2)]

        pA2 = S(nc.psum_tensor("pA2", [128, 512], F32))
        pB2 = S(nc.psum_tensor("pB2", [128, 512], F32))
        pC2 = S(nc.psum_tensor("pC2", [128, 512], F32))
        pD2 = S(nc.psum_tensor("pD2", [128, 512], F32))
        pRt = S(nc.psum_tensor("pRt", [128, E], F32))
        pCA = S(nc.psum_tensor("pCA", [1, E], F32))

        esB1 = ExitStack()
        B1 = esB1.enter_context
        lstTs = B1(nc.sbuf_tensor("lstTs", [128, 128], F32))
        clocbs = B1(nc.sbuf_tensor("clocbs", [128, E], F32))
        wgts = B1(nc.sbuf_tensor("wgts", [128, 8, E], BF16))
        xnlt = B1(nc.sbuf_tensor("xnlt", [128, 4, DM], F32))
        l2bf = [B1(nc.sbuf_tensor(f"l2bf{i}", [128, DM], BF16)) for i in range(2)]
        l2T = B1(nc.sbuf_tensor("l2T", [128, 8, 512], BF16))
        gsc = B1(nc.sbuf_tensor("gsc", [128, E], F32))
        gtmp = B1(nc.sbuf_tensor("gtmp", [128, E], F32))
        gm8 = B1(nc.sbuf_tensor("gm8", [128, 8], F32))
        ggs = B1(nc.sbuf_tensor("ggs", [128, 8], F32))
        gkeep = B1(nc.sbuf_tensor("gkeep", [128, 8], F32))
        amask = B1(nc.sbuf_tensor("amask", [128, E], F32))
        dg = B1(nc.sbuf_tensor("dg", [128, E], F32))
        dwork = B1(nc.sbuf_tensor("dwork", [128, E], F32))
        rmat = B1(nc.sbuf_tensor("rmat", [128, E], F32))
        dcol = [B1(nc.sbuf_tensor(f"dcol{i}", [128, 4], F32)) for i in range(2)]
        du32 = [B1(nc.sbuf_tensor(f"du32{i}", [128, 4], U32)) for i in range(2)]
        rec2 = [B1(nc.sbuf_tensor(f"rec2{i}", [128, 8], F32)) for i in range(2)]
        swb = B1(nc.sbuf_tensor("swb", [128, 8, 3 * DM], BF16))
        shT = B1(nc.sbuf_tensor("shT", [128, 8, 512], BF16))
        pT2 = B1(nc.psum_tensor("pT2", [128, 512], BF16))

        for name, dst in [("lstT", lstTs), ("clocb", clocbs),
                          ("iota8", iota8s), ("maskc", maskcs)]:
            P.op("sync", lambda e, a=ap[name], d=dst: e.dma_start(out=d[:], in_=a[:, :]), ld, 16)
        P.op("gpsimd", lambda e: e.dma_start(
            out=wgts[:], in_=wgt[:, :].rearrange("(k p) n -> p k n", p=128)), gq, 16)
        P.op("gpsimd", lambda e: e.dma_start(
            out=swb[:, :, 0:DM], in_=sw1[:, :].rearrange("(k p) n -> p k n", p=128)), gq, 16)
        P.op("gpsimd", lambda e: e.dma_start(
            out=swb[:, :, DM:2 * DM], in_=sw3[:, :].rearrange("(k p) n -> p k n", p=128)), gq, 16)
        P.op("gpsimd", lambda e: e.dma_start(
            out=swb[:, :, 2 * DM:3 * DM], in_=sw2[:, :].rearrange("(k p) n -> p k n", p=128)), gq, 16)
        P.op("vector", lambda e: e.memset(runA[:], 0.0), vs)
        P.wait("tensor", ld, P.val(ld))
        P.wait("tensor", gq, P.val(gq))
        P.wait("tensor", vs, P.val(vs))
        P.wait("vector", ld, P.val(ld))
        P.wait("vector", gq, P.val(gq))
        P.wait("gpsimd", sq, zval)

        def ln2_gate_route(jg):
            par = jg % 2
            jl = jg % 4
            P.op("sync", lambda e, jg=jg: e.dma_start(
                out=xnlt[:, jl, :], in_=xnw[jg * 128:(jg + 1) * 128, :]), ld, 16)
            P.wait("vector", ld, P.val(ld))
            P.wait("vector", gq, P.val(gq))
            P.op("vector", lambda e, jl=jl: e.tensor_reduce(
                out=cols[:, 0:1], in_=xnlt[:, jl, :], axis=AX, op=ALU.add), vs)
            P.op("vector", lambda e: e.tensor_scalar_mul(out=cols[:, 0:1], in0=cols[:, 0:1], scalar1=1.0 / DM), vs)
            P.op("vector", lambda e, jl=jl: e.tensor_scalar(
                out=hwork[:], in0=xnlt[:, jl, :], scalar1=cols[:, 0:1], scalar2=None, op0=ALU.subtract), vs)
            P.wait("scalar", vs, P.val(vs))
            P.op("scalar", lambda e: e.activation(
                out=arena[:, 0:DM], in_=hwork[:], func=AF.Square, accum_out=cols[:, 1:2]), ss)
            P.op("scalar", lambda e: e.activation(
                out=cols[:, 2:3], in_=cols[:, 1:2], func=AF.Ln, scale=1.0 / DM, bias=epsc[:, 0:1]), ss)
            P.op("scalar", lambda e: e.activation(
                out=cols[:, 2:3], in_=cols[:, 2:3], func=AF.Exp, scale=-0.5), ss)
            P.wait("vector", ss, P.val(ss))
            P.op("vector", lambda e, par=par: e.tensor_scalar(
                out=l2bf[par][:], in0=hwork[:], scalar1=cols[:, 2:3], scalar2=None, op0=ALU.mult), vs)
            P.wait("tensor", vs, P.val(vs))
            for k0 in (0, 4):
                P.guard("pT2")
                for i in range(4):
                    P.op("tensor", lambda e, k0=k0, i=i, par=par: e.transpose(
                        out=pT2[:, i * 128:(i + 1) * 128],
                        in_=l2bf[par][:, (k0 + i) * 128:(k0 + i + 1) * 128],
                        identity=identbs[:]), ts if i == 3 else None)
                P.wait("vector", ts, P.val(ts))
                P.op("vector", lambda e, k0=k0, jl=jl: e.tensor_copy(
                    out=l2T[:, k0:k0 + 4, jl * 128:(jl + 1) * 128],
                    in_=pT2[:, 0:512].rearrange("p (k x) -> p k x", k=4)), vs)
                P.evac("pT2", vs)
            P.wait("tensor", vs, P.val(vs))
            P.guard("pRt")
            for k in range(8):
                P.op("tensor", lambda e, k=k, jl=jl: e.matmul(
                    out=pRt[:, :], lhsT=l2T[:, k, jl * 128:(jl + 1) * 128], rhs=wgts[:, k, :],
                    start=(k == 0), stop=(k == 7)), ts if k == 7 else None)
            P.wait("vector", ts, P.val(ts))
            P.op("vector", lambda e: e.tensor_reduce(
                out=cols[:, 0:1], in_=pRt[:, :], axis=AX, op=ALU.max, negate=True), vs)
            P.wait("scalar", vs, P.val(vs))
            P.op("scalar", lambda e: e.activation(out=gsc[:], in_=pRt[:, :], func=AF.Exp, bias=cols[:, 0:1]), ss)
            P.evac("pRt", ss)
            P.wait("vector", ss, P.val(ss))
            P.op("vector", lambda e: e.tensor_reduce(out=cols[:, 1:2], in_=gsc[:], axis=AX, op=ALU.add), vs)
            P.op("vector", lambda e: e.reciprocal(out=cols[:, 2:3], in_=cols[:, 1:2]), vs)
            P.op("vector", lambda e: e.tensor_scalar(
                out=gsc[:], in0=gsc[:], scalar1=cols[:, 2:3], scalar2=None, op0=ALU.mult), vs)
            P.op("vector", lambda e: e.tensor_reduce(
                out=ggs[:].rearrange("p g -> p g ()"),
                in_=gsc[:].rearrange("p (g x) -> p g x", g=8), axis=AX, op=ALU.max), vs)
            P.op("vector", lambda e: e.max(out=gm8[:], in_=ggs[:]), vs)
            P.op("vector", lambda e: e.tensor_scalar(
                out=gkeep[:], in0=ggs[:], scalar1=gm8[:, 3:4], scalar2=None, op0=ALU.is_ge), vs)
            P.op("vector", lambda e: e.tensor_tensor(
                out=gtmp[:].rearrange("p (g x) -> p g x", g=8),
                in0=gsc[:].rearrange("p (g x) -> p g x", g=8),
                in1=gkeep[:].rearrange("p g -> p g ()").broadcast_to((128, 8, 4)), op=ALU.mult), vs)
            P.op("vector", lambda e: e.tensor_scalar(
                out=gkeep[:], in0=gkeep[:], scalar1=0.5, scalar2=None, op0=ALU.is_lt), vs)
            P.op("vector", lambda e: e.scalar_tensor_tensor(
                out=gtmp[:].rearrange("p (g x) -> p g x", g=8),
                in0=gkeep[:].rearrange("p g -> p g ()").broadcast_to((128, 8, 4)),
                scalar=-1e30, in1=gtmp[:].rearrange("p (g x) -> p g x", g=8),
                op0=ALU.mult, op1=ALU.add), vs)
            P.op("vector", lambda e: e.max(out=gm8[:], in_=gtmp[:]), vs)
            P.op("vector", lambda e: e.tensor_scalar(
                out=amask[:], in0=gtmp[:], scalar1=gm8[:, 3:4], scalar2=None, op0=ALU.is_ge), vs)
            P.op("vector", lambda e: e.tensor_tensor(out=rmat[:], in0=amask[:], in1=gsc[:], op=ALU.mult), vs)
            P.op("vector", lambda e: e.tensor_scalar(
                out=dwork[:], in0=rmat[:], scalar1=1.0, scalar2=0.0, op0=ALU.mult, op1=ALU.add,
                accum_out=cols[:, 3:4]), vs)
            P.op("vector", lambda e: e.tensor_scalar_add(out=cols[:, 3:4], in0=cols[:, 3:4], scalar1=1e-9), vs)
            P.op("vector", lambda e: e.reciprocal(out=cols[:, 4:5], in_=cols[:, 3:4]), vs)
            P.op("vector", lambda e: e.tensor_scalar(
                out=rmat[:], in0=rmat[:], scalar1=cols[:, 4:5], scalar2=None, op0=ALU.mult), vs)
            P.wait("tensor", vs, P.val(vs))
            P.guard("pRt")
            P.op("tensor", lambda e: e.matmul(out=pRt[:, :], lhsT=lstTs[:, :], rhs=amask[:],
                                              start=True, stop=False), None)
            P.op("tensor", lambda e: e.matmul(out=pRt[:, :], lhsT=onesrs[:, :],
                                              rhs=runA[:, :], start=False, stop=True), ts)
            P.guard("pCA")
            P.op("tensor", lambda e: e.matmul(
                out=pCA[:, :], lhsT=onecs[:, 0:1], rhs=amask[:],
                start=True, stop=True), ts)
            P.wait("vector", ts, P.val(ts))
            P.op("vector", lambda e: e.tensor_scalar(
                out=dwork[:], in0=pRt[:, :], scalar1=float(C_LOC), scalar2=None, op0=ALU.is_ge), vs)
            P.op("vector", lambda e: e.tensor_tensor(out=dg[:], in0=pRt[:, :], in1=clocbs[:], op=ALU.add), vs)
            P.evac("pRt", vs)
            P.op("vector", lambda e: e.tensor_tensor(out=runA[:], in0=runA[:], in1=pCA[:, :], op=ALU.add), vs)
            P.evac("pCA", vs)
            P.op("vector", lambda e: e.scalar_tensor_tensor(
                out=dg[:], in0=dwork[:], scalar=DUMP, in1=dg[:], op0=ALU.mult, op1=ALU.add), vs)
            for kk in range(4):
                P.op("vector", lambda e, kk=kk, par=par: e.scalar_tensor_tensor(
                    out=dwork[:], in0=gtmp[:], scalar=gm8[:, kk:kk + 1], in1=dg[:],
                    op0=ALU.is_equal, op1=ALU.mult, accum_out=dcol[par][:, kk:kk + 1]), vs)
                P.op("vector", lambda e, kk=kk, par=par: e.scalar_tensor_tensor(
                    out=dwork[:], in0=gtmp[:], scalar=gm8[:, kk:kk + 1], in1=rmat[:],
                    op0=ALU.is_equal, op1=ALU.mult, accum_out=rec2[par][:, 2 * kk:2 * kk + 1]), vs)
                P.op("vector", lambda e, kk=kk, par=par, jg=jg: e.tensor_scalar_add(
                    out=rec2[par][:, 2 * kk + 1:2 * kk + 2], in0=iota8s[:, 0:1],
                    scalar1=float(jg * 128)), vs)
            P.op("vector", lambda e, par=par: e.tensor_scalar_min(
                out=dcol[par][:], in0=dcol[par][:], scalar1=float(EB_ROWS - 1)), vs)
            P.op("vector", lambda e, par=par: e.tensor_copy(out=du32[par][:], in_=dcol[par][:]), vs)
            P.wait("gpsimd", vs, P.val(vs))
            for kk in range(4):
                P.op("gpsimd", lambda e, kk=kk, par=par: e.indirect_dma_start(
                    out=ebuf.ap(), out_offset=IndirectOffsetOnAxis(du32[par][:, kk:kk + 1], 0),
                    in_=l2bf[par][:, :], in_offset=None), gq, 16)
                P.op("gpsimd", lambda e, kk=kk, par=par: e.indirect_dma_start(
                    out=rec.ap(), out_offset=IndirectOffsetOnAxis(du32[par][:, kk:kk + 1], 0),
                    in_=rec2[par][:, 2 * kk:2 * kk + 2], in_offset=None), gq, 16)

        def shared_lt(lt):
            for ic in range(8):
                P.wait("tensor", vs, P.val(vs))
                P.guard("pA2")
                P.guard("pB2")
                for k in range(8):
                    P.op("tensor", lambda e, ic=ic, k=k: e.matmul(
                        out=pA2[:, 0:512], lhsT=swb[:, k, ic * 128:ic * 128 + 128],
                        rhs=l2T[:, k, :], start=(k == 0), stop=(k == 7)), None)
                for k in range(8):
                    P.op("tensor", lambda e, ic=ic, k=k: e.matmul(
                        out=pB2[:, 0:512], lhsT=swb[:, k, DM + ic * 128:DM + ic * 128 + 128],
                        rhs=l2T[:, k, :], start=(k == 0), stop=(k == 7)),
                        ts if k == 7 else None)
                P.wait("scalar", ts, P.val(ts))
                P.op("scalar", lambda e, ic=ic: e.activation(out=arena[:, 0:512], in_=pA2[:, 0:512], func=AF.Silu), ss)
                P.evac("pA2", ss)
                P.wait("vector", ss, P.val(ss))
                P.op("vector", lambda e, ic=ic: e.tensor_tensor(
                    out=shT[:, ic, :], in0=arena[:, 0:512], in1=pB2[:, 0:512], op=ALU.mult), vs)
                P.evac("pB2", vs)
            P.wait("tensor", vs, P.val(vs))
            for j2 in range(4):
                for n in range(2):
                    P.guard("pC2")
                    for k in range(8):
                        P.op("tensor", lambda e, j2=j2, n=n, k=k: e.matmul(
                            out=pC2[:, 0:512], lhsT=shT[:, k, j2 * 128:j2 * 128 + 128],
                            rhs=swb[:, k, 2 * DM + n * 512:2 * DM + (n + 1) * 512],
                            start=(k == 0), stop=(k == 7)), ts if k == 7 else None)
                    P.wait("vector", ts, P.val(ts))
                    P.wait("vector", ost, P.val(ost))
                    P.op("vector", lambda e, j2=j2, n=n: e.tensor_tensor(
                        out=ysc[0][:, n * 512:(n + 1) * 512], in0=pC2[:, 0:512],
                        in1=xnlt[:, j2, n * 512:(n + 1) * 512], op=ALU.add), vs)
                    P.evac("pC2", vs)
                P.wait("sync", vs, P.val(vs))
                P.op("sync", lambda e, lt=lt, j2=j2: e.dma_start(
                    out=ap["out"][lt * 512 + j2 * 128:lt * 512 + (j2 + 1) * 128, :], in_=ysc[0][:]), ost, 16)

        for lt in range(8):
            for j in range(4):
                ln2_gate_route(lt * 4 + j)
            if lt == 7:
                P.wait("sync", vs, P.val(vs))
                P.op("sync", lambda e: e.dma_start(out=cntin.ap()[:, :], in_=runA[:]), ld, 16)
                P.wait("gpsimd", ld, P.val(ld))
                P.op("gpsimd", lambda e: e.collective_compute(
                    "AllGather", ALU.bypass,
                    replica_groups=[[0, 1, 2, 3, 4, 5, 6, 7]],
                    ins=[cntin.ap()[:, :].opt()], outs=[cnt8o.ap()[:, :].opt()]), "ccs", 1)
                cdone = P.val("ccs")
            shared_lt(lt)

        P.wait("sync", "ccs", cdone)
        P.op("sync", lambda e: e.dma_start(out=cnt8s[:], in_=cnt8o.ap()[:, :]), ld, 16)
        P.wait("tensor", ld, P.val(ld))
        P.guard("pCA")
        P.op("tensor", lambda e: e.matmul(out=pCA[:, :], lhsT=maskcs[:, 0:1], rhs=cnt8s[:, :],
                                          start=True, stop=True), ts)
        P.wait("vector", ts, P.val(ts))
        P.op("vector", lambda e: e.tensor_copy(out=bases[:], in_=pCA[:, :]), vs)
        P.evac("pCA", vs)
        P.wait("tensor", vs, P.val(vs))
        P.guard("pRt")
        P.op("tensor", lambda e: e.matmul(out=pRt[:, :], lhsT=onesrs[0:1, :], rhs=bases[0:1, :],
                                          start=True, stop=True), ts)
        P.wait("vector", ts, P.val(ts))
        P.op("vector", lambda e: e.tensor_scalar(
            out=basen[:], in0=pRt[:, :], scalar1=-1.0, scalar2=float(CAP),
            op0=ALU.mult, op1=ALU.add), vs)
        P.evac("pRt", vs)

        for eng in ENGS:
            for s in (ld, gq, sq, ts, vs, ss):
                P.wait(eng, s)
        esB1.close()

        w1b = [S(nc.sbuf_tensor(f"w1b{i}", [128, 8, DM], BF16)) for i in range(2)]
        w3b = [S(nc.sbuf_tensor(f"w3b{i}", [128, 8, DM], BF16)) for i in range(2)]
        w2b = [S(nc.sbuf_tensor(f"w2b{i}", [128, 8, DM], BF16)) for i in range(2)]
        xerows = [S(nc.sbuf_tensor(f"xerows{i}", [128, 8, DM], BF16)) for i in range(2)]
        xeT = S(nc.sbuf_tensor("xeT", [128, 8, C_LOC], BF16))
        hT = S(nc.sbuf_tensor("hT", [128, 8, C_LOC], BF16))
        pTa = S(nc.psum_tensor("pTa", [128, 1024], BF16))
        pTb = S(nc.psum_tensor("pTb", [128, 1024], BF16))

        ldv, sqv, gqv, tsdone = {}, {}, {}, {}
        scv = [0, 0]

        def issue_loads(e2):
            par = e2 % 2
            if e2 >= 2:
                g = tsdone[e2 - 2]
                P.wait("sync", ts, g)
                P.wait("scalar", ts, g)
                P.wait("gpsimd", ts, g)
            P.op("sync", lambda e, e2=e2, par=par: e.dma_start(
                out=w1b[par][:], in_=w1_i[e2 * 128:(e2 + 1) * 128, :].rearrange(
                    "p (k n) -> p k n", k=8)), ld, 16)
            ldv[e2] = P.val(ld)
            P.op("scalar", lambda e, e2=e2, par=par: e.dma_start(
                out=w3b[par][:], in_=w3_i[e2 * 128:(e2 + 1) * 128, :].rearrange(
                    "p (k n) -> p k n", k=8)), sq, 16)
            P.op("scalar", lambda e, e2=e2, par=par: e.dma_start(
                out=xerows[par][:], in_=ebuf[e2 * C_LOC:(e2 + 1) * C_LOC, :].rearrange(
                    "(s p) n -> p s n", p=128)), sq, 16)
            sqv[e2] = P.val(sq)
            P.op("gpsimd", lambda e, e2=e2, par=par: e.dma_start(
                out=w2b[par][:], in_=w2_i[e2 * 128:(e2 + 1) * 128, :].rearrange(
                    "p (k n) -> p k n", k=8)), gq, 16)
            gqv[e2] = P.val(gq)

        issue_loads(0)
        issue_loads(1)
        for eix in range(E):
            par = eix % 2
            w1c, w3c, w2c = w1b[par], w3b[par], w2b[par]
            P.op("vector", lambda e, eix=eix: e.tensor_scalar(
                out=ind8[:], in0=iota8s[:], scalar1=basen[:, eix:eix + 1], scalar2=None,
                op0=ALU.is_lt), vs)
            P.wait("tensor", ld, ldv[eix])
            P.wait("tensor", sq, sqv[eix])
            P.wait("tensor", gq, gqv[eix])
            for kc in range(8):
                bank = pTa if kc % 2 == 0 else pTb
                P.guard("pTa" if kc % 2 == 0 else "pTb")
                for s in range(8):
                    P.op("tensor", lambda e, kc=kc, s=s, par=par, bank=bank: e.transpose(
                        out=bank[:, s * 128:(s + 1) * 128],
                        in_=xerows[par][:, s, kc * 128:(kc + 1) * 128],
                        identity=identbs[:]), ts if s == 7 else None)
                P.wait("vector", ts, P.val(ts))
                P.op("vector", lambda e, kc=kc, bank=bank: e.tensor_copy(
                    out=xeT[:, kc, :], in_=bank[:, :]), vs)
                P.evac("pTa" if kc % 2 == 0 else "pTb", vs)
            P.wait("tensor", vs, P.val(vs))
            for ic in range(8):
                for nn in range(2):
                    g = (ic * 2 + nn) % 2
                    b1, n1, b2, n2 = (pA2, "pA2", pB2, "pB2") if g == 0 else (pC2, "pC2", pD2, "pD2")
                    ar = arena[:, g * 512:(g + 1) * 512]
                    P.guard(n1)
                    P.guard(n2)
                    for k in range(8):
                        P.op("tensor", lambda e, ic=ic, nn=nn, k=k, w1c=w1c, b1=b1: e.matmul(
                            out=b1[:, 0:512], lhsT=w1c[:, k, ic * 128:ic * 128 + 128],
                            rhs=xeT[:, k, nn * 512:(nn + 1) * 512], start=(k == 0), stop=(k == 7)), None)
                    for k in range(8):
                        P.op("tensor", lambda e, ic=ic, nn=nn, k=k, w3c=w3c, b2=b2: e.matmul(
                            out=b2[:, 0:512], lhsT=w3c[:, k, ic * 128:ic * 128 + 128],
                            rhs=xeT[:, k, nn * 512:(nn + 1) * 512], start=(k == 0), stop=(k == 7)),
                            ts if k == 7 else None)
                    P.wait("scalar", ts, P.val(ts))
                    P.op("scalar", lambda e, b1=b1, ar=ar: e.activation(out=ar, in_=b1[:, 0:512], func=AF.Silu), ss)
                    P.evac(n1, ss)
                    P.wait("vector", ss, P.val(ss))
                    P.op("vector", lambda e, ic=ic, nn=nn, b2=b2, ar=ar: e.tensor_tensor(
                        out=hT[:, ic, nn * 512:(nn + 1) * 512], in0=ar,
                        in1=b2[:, 0:512], op=ALU.mult), vs)
                    P.evac(n2, vs)
            P.wait("tensor", vs, P.val(vs))
            for rt in range(8):
                g = rt % 2
                b1, n1, b2, n2 = (pA2, "pA2", pB2, "pB2") if g == 0 else (pC2, "pC2", pD2, "pD2")
                P.guard(n1)
                P.guard(n2)
                for nn in range(2):
                    for k in range(8):
                        P.op("tensor", lambda e, rt=rt, nn=nn, k=k, w2c=w2c, b1=b1, b2=b2: e.matmul(
                            out=(b1 if nn == 0 else b2)[:, 0:512],
                            lhsT=hT[:, k, rt * 128:rt * 128 + 128],
                            rhs=w2c[:, k, nn * 512:(nn + 1) * 512],
                            start=(k == 0), stop=(k == 7)), ts if k == 7 else None)
                P.wait("sync", vs, P.val(vs))
                P.op("sync", lambda e, eix=eix, rt=rt, g=g: e.dma_start(
                    out=recs[g][:], in_=rec[eix * C_LOC + rt * 128:eix * C_LOC + (rt + 1) * 128, :]), ld, 16)
                P.wait("vector", gq, scv[g])
                P.wait("vector", ts, P.val(ts))
                P.wait("vector", ld, P.val(ld))
                P.op("vector", lambda e, rt=rt, g=g, b1=b1: e.tensor_scalar(
                    out=ysc[g][:, 0:512], in0=b1[:, 0:512], scalar1=recs[g][:, 0:1],
                    scalar2=ind8[:, rt:rt + 1], op0=ALU.mult, op1=ALU.mult), vs)
                P.op("vector", lambda e, rt=rt, g=g, b2=b2: e.tensor_scalar(
                    out=ysc[g][:, 512:1024], in0=b2[:, 0:512], scalar1=recs[g][:, 0:1],
                    scalar2=ind8[:, rt:rt + 1], op0=ALU.mult, op1=ALU.mult), vs)
                P.evac(n1, vs)
                P.evac(n2, vs)
                P.op("vector", lambda e, g=g: e.tensor_copy(out=tu32[g][:], in_=recs[g][:, 1:2]), vs)
                P.wait("gpsimd", vs, P.val(vs))
                P.op("gpsimd", lambda e, g=g: e.indirect_dma_start(
                    out=ap["out"], out_offset=IndirectOffsetOnAxis(tu32[g][:, 0:1], 0),
                    in_=ysc[g][:, :], in_offset=None, compute_op=ALU.add), gq, 16)
                scv[g] = P.val(gq)
            tsdone[eix] = P.val(ts)
            if eix + 2 < E:
                issue_loads(eix + 2)

        P.wait("sync", gq, P.val(gq))
        P.wait("sync", ld, P.val(ld))
        P.wait("sync", ost, P.val(ost))

        block = S(nc.Block())
        P.run(block)
    return nc


def host_inputs(inputs):
    x = np.ascontiguousarray(inputs["L0_attn_hidden_states"].reshape(4 * 8192, DM))
    W_in = inputs["L0_attn_W_in"]
    winp = np.zeros((DM, DIP), np.float32)
    winp[:, :4384] = W_in
    conv_w = inputs["L0_attn_conv_w"]
    convd = np.zeros((4 * NCT * 128, 128), np.float32)
    for tap in range(4):
        for ct in range(NCT):
            blk = convd[(tap * NCT + ct) * 128:(tap * NCT + ct + 1) * 128]
            np.fill_diagonal(blk, conv_w[ct * 128:(ct + 1) * 128, tap])
    convb = np.ascontiguousarray(inputs["L0_attn_conv_b"].reshape(NCT, 128).T)
    dtb = inputs["L0_attn_dt_bias"].reshape(32, 1)
    alog = inputs["L0_attn_A_log"].reshape(32, 1)
    dbc = np.broadcast_to(np.repeat(inputs["L0_attn_D"], 64)[None, :], (128, DI)).copy()
    rmsbc = np.broadcast_to(inputs["L0_attn_rms_w"][None, :], (128, DI)).copy()
    lngbc = np.broadcast_to(inputs["L0_attn_ln_g"][None, :], (128, DM)).copy()
    lnbbc = np.broadcast_to(inputs["L0_attn_ln_b"][None, :], (128, DM)).copy()
    wout = inputs["L0_attn_W_out"]
    wgt = np.ascontiguousarray(inputs["L0_ffn_wg"].T)
    diagm = np.zeros((128, 8 * 128), ml_dtypes.bfloat16)
    for h in range(8):
        np.fill_diagonal(diagm[:, h * 128:(h + 1) * 128], ml_dtypes.bfloat16(1.0))
    bigm = np.where(np.arange(128)[:, None] <= np.arange(128)[None, :], 0.0, -30000.0).astype(np.float32)
    ones128 = np.ones((128, 128), np.float32)
    identf = np.eye(128, dtype=np.float32)
    identb = np.eye(128, dtype=ml_dtypes.bfloat16)
    onesr = np.ones((1, 128), np.float32)
    onescol = np.ones((128, 1), np.float32)
    oh127 = np.zeros((128, 1), np.float32)
    oh127[127, 0] = 1.0
    w1 = np.ascontiguousarray(inputs["L0_ffn_w1"].reshape(E * DM, DM))
    w3 = np.ascontiguousarray(inputs["L0_ffn_w3"].reshape(E * DM, DM))
    w2 = np.ascontiguousarray(inputs["L0_ffn_w2"].reshape(E * DM, DM))
    lstT = np.where(np.arange(128)[:, None] < np.arange(128)[None, :], 1.0, 0.0).astype(np.float32)
    clocb = np.broadcast_to((np.arange(E) * C_LOC).astype(np.float32)[None, :], (128, E)).copy()
    iota8 = (np.arange(128, dtype=np.float32)[:, None]
             + 128.0 * np.arange(8, dtype=np.float32)[None, :]).copy()
    maps = []
    for c in range(NCORES):
        b, half = c // 2, c % 2
        base = b * 8192 + half * NTOK
        xo = np.zeros((NTOK + 3, DM), np.float32)
        xo[3:] = x[base:base + NTOK]
        if half == 1:
            xo[:3] = x[base - 3:base]
        xpred = np.zeros((NTOK + 3, DM), np.float32)
        if half == 1:
            xpred[3:] = x[b * 8192:b * 8192 + NTOK]
        selc = np.full((128, 1), float(half), np.float32)
        maskc = (np.arange(8) < c).astype(np.float32).reshape(8, 1)
        maps.append({
            "xo": xo, "xp": xpred, "win": winp, "convd": convd, "convb": convb,
            "dtb": dtb, "alog": alog, "dbc": dbc, "rmsbc": rmsbc, "lngbc": lngbc,
            "lnbbc": lnbbc, "wout": wout, "selc": selc, "diagm": diagm,
            "bigm": bigm, "ones128": ones128, "identf": identf, "identb": identb,
            "onesr": onesr, "onescol": onescol, "oh127": oh127,
            "wgt": wgt, "w1": w1, "w3": w3, "w2": w2,
            "sw1": inputs["L0_ffn_sw1"], "sw3": inputs["L0_ffn_sw3"], "sw2": inputs["L0_ffn_sw2"],
            "lstT": lstT, "clocb": clocb, "iota8": iota8, "maskc": maskc,
        })
    return maps


def kernel(**inputs):
    maps = host_inputs(inputs)
    nc = build()
    res = run_bass_kernel_spmd(nc, maps, core_ids=list(range(NCORES)))
    out = np.zeros((4, 8192, DM), np.float32)
    for c in range(NCORES):
        b, half = c // 2, c % 2
        out[b, half * NTOK:(half + 1) * NTOK] = np.asarray(res.results[c]["out"])
    return out
